# version 15
# baseline (speedup 1.0000x reference)
import numpy as np

import concourse.bass as bass
import concourse.mybir as mybir
from concourse.bass_utils import run_bass_kernel_spmd

F32 = mybir.dt.float32
BF16 = mybir.dt.bfloat16
F16 = mybir.dt.float16
U16 = mybir.dt.uint16
AF = mybir.ActivationFunctionType
ALU = mybir.AluOpType
AX = mybir.AxisListType

TOK = 8192
DM = 1024
NST = 256
DI = 3072
ND = DM // 128
NI = DI // 128
NH = NST // 128
CA = 512
NCA = TOK // CA
CC = 256
NCC = TOK // CC
QMAX = 32767.0
SHIFT = 32768.0
ROFF = 32768.5

ENG_SEM = {
    "sync": ("dS", 16, 8),
    "synct": ("dT", 16, 4),
    "scalt": ("dAT", 16, 4),
    "gdma": ("dG", 16, 8),
    "tensor": ("cT", 1, 1),
    "vector": ("cV", 1, 1),
    "scalar": ("cS", 1, 1),
    "gpsimd": ("cG", 1, 1),
}


def sem_names():
    out = []
    for base, step, pool in ENG_SEM.values():
        if pool == 1:
            out.append(base)
        else:
            out += [f"{base}{i}" for i in range(pool)]
    return out


class Rec:

    def __init__(self):
        self.ops = []
        self.counts = {s: 0 for s in sem_names()}
        self.rr = {k: 0 for k in ENG_SEM}
        self.events = {}

    def op(self, eng, fn, sig=None, wait=()):
        base, step, pool = ENG_SEM[eng]
        if sig is not None:
            if pool == 1:
                sem = base
            else:
                sem = f"{base}{self.rr[eng] % pool}"
                self.rr[eng] += 1
            self.counts[sem] += step
            assert sig not in self.events, sig
            self.events[sig] = (sem, self.counts[sem])
        if isinstance(wait, str):
            wait = (wait,)
        self.ops.append((eng, fn, sig, tuple(wait)))

    def emit(self, block, sems, prefix=None):
        ops = self.ops if prefix is None else self.ops[:prefix]

        def body_for(keys):
            def body(eng):
                last = {}
                for key, fn, sig, waits in ops:
                    if key not in keys:
                        continue
                    for w in waits:
                        s, v = self.events[w]
                        if last.get(s, 0) < v:
                            eng.wait_ge(sems[s], v)
                            last[s] = v
                    instr = fn(eng)
                    if sig is not None:
                        sem, _ = self.events[sig]
                        _, step, _ = ENG_SEM[key]
                        instr.then_inc(sems[sem], step)
            return body

        block.sync(body_for(("sync", "synct")))
        block.tensor(body_for(("tensor",)))
        block.vector(body_for(("vector",)))
        block.scalar(body_for(("scalar", "scalt")))
        block.gpsimd(body_for(("gpsimd", "gdma")))


def s0ap(handle, offset, pstride, pcount, free):
    return bass.AP(tensor=handle.ap().tensor, offset=offset,
                   ap=[[pstride, pcount]] + list(free))


def build(dbg=(), max_phase=4, prefix=None):
    nc = bass.Bass("TRN2", target_bir_lowering=False)
    dbg = set(dbg)

    u_h = nc.dram_tensor("u", [TOK, DM], F32, kind="ExternalInput").ap()
    ll_h = nc.dram_tensor("ll", [NST], F32, kind="ExternalInput").ap()
    bw_h = nc.dram_tensor("bw", [NST, DM], F32, kind="ExternalInput").ap()
    cw_h = nc.dram_tensor("cw", [DM, NST], F32, kind="ExternalInput").ap()
    dv_h = nc.dram_tensor("dv", [DM], F32, kind="ExternalInput").ap()
    wsl_h = nc.dram_tensor("wslice", [9 * 128, DM], F32,
                           kind="ExternalInput").ap()
    out_h = nc.dram_tensor("out", [TOK, DM], F32, kind="ExternalOutput").ap()

    def scratch(name, shape, dtype):
        kind = "ExternalOutput" if name in dbg else "Internal"
        return nc.dram_tensor(name, shape, dtype, kind=kind)

    x1s = scratch("x1s", [DM, TOK], F16)
    uis = scratch("uis", [DI, TOK], F16)
    gis = scratch("gis", [DI, TOK], F16)
    g2s = scratch("g2s", [DI, TOK], F16)
    ops_ = scratch("ops", [DI, TOK], F16)
    NT_W = 72
    wag_in = nc.dram_tensor("wag_in", [NT_W // 8 * 128, DM], F16)
    wag = scratch("wag", [NT_W * 128, DM], F16)
    cc_bufs = [(nc.dram_tensor(f"cci{i}", [1, 8], F32),
                nc.dram_tensor(f"cco{i}", [1, 8], F32)) for i in range(4)]
    if "wagdbg" in dbg:
        wag_dbg = nc.dram_tensor("wag_dbg", [NT_W * 128, DM], F16,
                                 kind="ExternalOutput")
    if "scal" in dbg:
        dbg_scal = nc.dram_tensor("scal_dbg", [1, 16], F32,
                                  kind="ExternalOutput")

    rec = Rec()
    from contextlib import ExitStack
    with ExitStack() as es:
        ec = es.enter_context
        sems = {}
        for s in sem_names():
            sems[s] = ec(nc.semaphore(s))

        P = [ec(nc.psum_tensor(f"P{i}", [128, 512], F32)) for i in range(8)]

        ident = ec(nc.sbuf_tensor("ident", [128, 128], F32))
        ones_mat = ec(nc.sbuf_tensor("ones_mat", [128, 128], F16))
        redm = ec(nc.sbuf_tensor("redm", [128, 128], F32))
        onesf = ec(nc.sbuf_tensor("onesf", [1, 128], F32))
        gm2 = ec(nc.sbuf_tensor("gm2", [128, 128], F32))
        ll_sb = ec(nc.sbuf_tensor("ll_sb", [128, NH], F32))
        lam_sb = ec(nc.sbuf_tensor("lam_sb", [128, NH], F32))
        tmp_lam = ec(nc.sbuf_tensor("tmp_lam", [128, NH], F32))
        dv_sb = ec(nc.sbuf_tensor("dv_sb", [128, ND], F32))
        onepD = ec(nc.sbuf_tensor("onepD", [128, ND], F32))

        iota_p = ec(nc.sbuf_tensor("iota_p", [128, 128], F32))
        iota_f = ec(nc.sbuf_tensor("iota_f", [128, 128], F32))
        rmax_x1 = ec(nc.sbuf_tensor("rmax_x1", [128, 2], F32))
        rmax_ui = ec(nc.sbuf_tensor("rmax_ui", [128, 2], F32))
        rmax_gi = ec(nc.sbuf_tensor("rmax_gi", [128, 2], F32))
        rmax_g2 = ec(nc.sbuf_tensor("rmax_g2", [128, 2], F32))
        rmax_o = ec(nc.sbuf_tensor("rmax_o", [128, 2], F32))
        acol = ec(nc.sbuf_tensor("acol", [128, 4], F32))
        redc = ec(nc.sbuf_tensor("redc", [128, 1], F32))
        gmax_row = ec(nc.sbuf_tensor("gmax_row", [1, 128], F32))
        cc_sb = ec(nc.sbuf_tensor("cc_sb", [1, 8], F32))
        ccr_sb = ec(nc.sbuf_tensor("ccr_sb", [1, 8], F32))
        pp1 = ec(nc.sbuf_tensor("pp1", [128, 8], F32))
        pp2 = ec(nc.sbuf_tensor("pp2", [128, 8], F32))
        pp3 = ec(nc.sbuf_tensor("pp3", [128, 8], F32))
        pp4 = ec(nc.sbuf_tensor("pp4", [128, 8], F32))
        sc_sb = ec(nc.sbuf_tensor("sc_sb", [1, 16], F32))
        eps_sb = ec(nc.sbuf_tensor("eps_sb", [128, 1], F32))

        esRun = ExitStack()
        runmx = esRun.enter_context(nc.sbuf_tensor("runmx", [128, 5632], F16))
        runmn = esRun.enter_context(nc.sbuf_tensor("runmn", [128, 5632], F16))
        esA = ExitStack()
        ea = esA.enter_context
        bw_nat = ea(nc.sbuf_tensor("bw_nat", [128, NH, DM], F32))
        cw_nat = ea(nc.sbuf_tensor("cw_nat", [128, ND, NST], F32))
        bwT = ea(nc.sbuf_tensor("bwT", [128, ND, NST], F16))
        cwT = ea(nc.sbuf_tensor("cwT", [128, NH, DM], BF16))

        rec.op("sync", lambda e: e.dma_start(
            out=ll_sb[:], in_=ll_h.rearrange("(j p) -> p j", p=128),
            allow_slow_non_contiguous=True), sig="ld_ll")
        rec.op("sync", lambda e: e.dma_start(
            out=dv_sb[:], in_=dv_h.rearrange("(k p) -> p k", p=128),
            allow_slow_non_contiguous=True), sig="ld_dv")
        rec.op("sync", lambda e: e.dma_start(
            out=bw_nat[:, :, :],
            in_=bw_h.rearrange("(j p) d -> p j d", p=128)), sig="ld_bw")
        rec.op("sync", lambda e: e.dma_start(
            out=cw_nat[:, :, :],
            in_=cw_h.rearrange("(k p) n -> p k n", p=128)), sig="ld_cw")

        rec.op("gpsimd", lambda e: e.iota(
            iota_p[:], [[0, 128]], base=0, channel_multiplier=1,
            allow_small_or_imprecise_dtypes=True), sig="iop")
        rec.op("gpsimd", lambda e: e.iota(
            iota_f[:], [[1, 128]], base=0, channel_multiplier=0,
            allow_small_or_imprecise_dtypes=True), sig="iof")
        rec.op("vector", lambda e: e.tensor_tensor(
            ident[:], iota_p[:], iota_f[:], ALU.is_equal),
            sig="ident", wait=("iop", "iof"))
        rec.op("vector", lambda e: e.memset(ones_mat[:], 1.0), sig="ones_c")
        rec.op("vector", lambda e: e.memset(redm[:], 0.0), sig="redmz")
        rec.op("vector", lambda e: e.memset(onesf[:], 1.0), sig="onesf")
        rec.op("vector", lambda e: e.memset(cc_sb[:], 0.0), sig="ccz")
        rec.op("vector", lambda e: e.memset(eps_sb[:], 1e-5), sig="epsz")
        for nm, t in (("z1", rmax_x1), ("z2", rmax_ui), ("z3", rmax_gi),
                      ("z4", rmax_g2), ("z5", rmax_o)):
            rec.op("vector", lambda e, t=t: e.memset(t[:], 0.0), sig=nm)
        rec.op("vector", lambda e: e.memset(runmx[:], -1e30), sig="zmx")
        rec.op("vector", lambda e: e.memset(runmn[:], 1e30), sig="zmn")

        rec.op("scalar", lambda e: e.activation(
            tmp_lam[:], ll_sb[:], AF.Exp, bias=0.0, scale=-1.0),
            sig="lamexp", wait="ld_ll")
        rec.op("vector", lambda e: e.tensor_scalar(
            tmp_lam[:], tmp_lam[:], 1.0, None, ALU.add),
            sig="lamadd", wait="lamexp")
        rec.op("vector", lambda e: e.reciprocal(lam_sb[:], tmp_lam[:]),
               sig="lam", wait="lamadd")
        rec.op("vector", lambda e: e.tensor_scalar(
            onepD[:], dv_sb[:], 1.0, None, ALU.add), sig="onepD",
            wait="ld_dv")

        tc = 0
        for j in range(NH):
            for k in range(ND):
                nmT, nmS = f"bwT_t{tc}", f"bwT_s{tc}"
                w = ["ld_bw", "ident"] + ([f"bwT_s{tc-1}"] if tc else [])
                rec.op("tensor", lambda e, j=j, k=k: e.transpose(
                    P[6][:, 0:128], bw_nat[:, j, k * 128:(k + 1) * 128],
                    ident[:]), sig=nmT, wait=tuple(w))
                rec.op("scalar", lambda e, j=j, k=k: e.copy(
                    bwT[:, k, j * 128:(j + 1) * 128], P[6][:, 0:128]),
                    sig=nmS, wait=nmT)
                tc += 1
        tc = 0
        for k in range(ND):
            for j in range(NH):
                nmT, nmS = f"cwT_t{tc}", f"cwT_s{tc}"
                w = ["ld_cw", "ident"] + ([f"cwT_s{tc-1}"] if tc else [])
                rec.op("tensor", lambda e, j=j, k=k: e.transpose(
                    P[7][:, 0:128], cw_nat[:, k, j * 128:(j + 1) * 128],
                    ident[:]), sig=nmT, wait=tuple(w))
                rec.op("scalar", lambda e, j=j, k=k: e.copy(
                    cwT[:, j, k * 128:(k + 1) * 128], P[7][:, 0:128]),
                    sig=nmS, wait=nmT)
                tc += 1

        wq_nat = ea(nc.sbuf_tensor("wq_nat", [128, 2, DM], F32))
        wq_bm = ea(nc.sbuf_tensor("wq_bm", [128, 2, 32], F32))
        wq_s = ea(nc.sbuf_tensor("wq_s", [128, 2, 32], F32))
        wq_si = ea(nc.sbuf_tensor("wq_si", [128, 2, 32], F32))
        wq_t = ea(nc.sbuf_tensor("wq_t", [128, 2, DM], F32))
        wq_u = ea(nc.sbuf_tensor("wq_u", [128, 2, DM], U16))
        wq_o = ea(nc.sbuf_tensor("wq_o", [128, 2, DM], F16))

        def dequant_tile(s):
            b = s % 2
            grp = []
            w = [f"wq_sp{s-2}"] if s >= 2 else []
            grp.append(("sync", lambda e, s=s, b=b: e.dma_start(
                out=wq_nat[:, b, :], in_=wsl_h[s * 128:(s + 1) * 128, :]),
                f"wq_ld{s}", tuple(w)))
            w2 = [f"wq_ld{s}"] + ([f"wq_si{s-2}"] if s >= 2 else [])
            grp.append(("vector", lambda e, b=b: e.tensor_reduce(
                wq_bm[:, b, :], wq_nat.ap().rearrange(
                    "p a (bl el) -> p a bl el", el=32)[:, b], AX.X, ALU.max,
                apply_absolute_value=True), f"wq_bm{s}", tuple(w2)))
            w3 = [f"wq_bm{s}"] + ([f"wq_o{s-2}"] if s >= 2 else [])
            grp.append(("vector", lambda e, b=b: e.tensor_scalar(
                wq_s[:, b, :], wq_bm[:, b, :], 1.0 / 7.0, 1e-12, ALU.mult,
                ALU.max), f"wq_s{s}", tuple(w3)))
            grp.append(("vector", lambda e, b=b: e.reciprocal(
                wq_si[:, b, :], wq_s[:, b, :]), f"wq_si{s}", (f"wq_s{s}",)))

            def mul_bcast(e, b=b):
                sib = s0ap(wq_si, b * 32, 64, 128, [[1, 32], [0, 32]])
                return e.tensor_tensor(
                    wq_t.ap().rearrange("p a (bl el) -> p a bl el",
                                        el=32)[:, b],
                    wq_nat.ap().rearrange("p a (bl el) -> p a bl el",
                                          el=32)[:, b],
                    sib, ALU.mult)
            w4 = [f"wq_si{s}"] + ([f"wq_q{s-2}"] if s >= 2 else [])
            grp.append(("vector", mul_bcast, f"wq_t{s}", tuple(w4)))
            WOFF = 8.0
            grp.append(("vector", lambda e, b=b: e.tensor_scalar(
                wq_u[:, b, :], wq_t[:, b, :], 1.0, WOFF, ALU.mult, ALU.add),
                f"wq_u{s}", (f"wq_t{s}",)))
            grp.append(("vector", lambda e, b=b: e.scalar_tensor_tensor(
                wq_nat[:, b, :], wq_u[:, b, :], -WOFF, wq_t[:, b, :],
                ALU.add, ALU.subtract), f"wq_e{s}", (f"wq_u{s}",)))
            grp.append(("vector", lambda e, b=b: e.tensor_scalar(
                wq_t[:, b, :], wq_nat[:, b, :], -0.5, None, ALU.is_le),
                f"wq_c{s}", (f"wq_e{s}",)))
            grp.append(("vector", lambda e, b=b: e.scalar_tensor_tensor(
                wq_nat[:, b, :], wq_u[:, b, :], -WOFF, wq_t[:, b, :],
                ALU.add, ALU.add), f"wq_q{s}", (f"wq_c{s}",)))

            def deq_mul(e, b=b):
                sb = s0ap(wq_s, b * 32, 64, 128, [[1, 32], [0, 32]])
                return e.tensor_tensor(
                    wq_o.ap().rearrange("p a (bl el) -> p a bl el",
                                        el=32)[:, b],
                    wq_nat.ap().rearrange("p a (bl el) -> p a bl el",
                                          el=32)[:, b],
                    sb, ALU.mult)
            grp.append(("vector", deq_mul, f"wq_o{s}", (f"wq_q{s}",)))
            grp.append(("gdma", lambda e, s=s, b=b: e.dma_start(
                out=wag_in[s * 128:(s + 1) * 128, :], in_=wq_o[:, b, :]),
                f"wq_sp{s}", (f"wq_o{s}",)))
            return grp

        dequant_ops = [dequant_tile(s) for s in range(9)]

        u_nat = ea(nc.sbuf_tensor("u_nat", [128, 2, 4, DM], F32))
        uT16 = ea(nc.sbuf_tensor("uT16", [128, 2, ND, CA], F16))
        h32 = ea(nc.sbuf_tensor("h32", [128, 2, NH, CA], F32))
        h16 = ea(nc.sbuf_tensor("h16", [128, 2, NH, CA], BF16))
        h16lo = ea(nc.sbuf_tensor("h16lo", [128, 2, NH, CA], BF16))
        x1pre = ea(nc.sbuf_tensor("x1pre", [128, 2, ND, CA], F16))
        x1sq = ea(nc.sbuf_tensor("x1sq", [128, ND, CA], F16))
        ab_b = ea(nc.sbuf_tensor("ab_b", [128, 2, 2, CA], F16))
        x1out = ea(nc.sbuf_tensor("x1out", [128, 2, ND, CA], F16))
        tmpa = ea(nc.sbuf_tensor("tmpa", [128, 2, CA], F16))
        stf = ea(nc.sbuf_tensor("stf", [128, 4, CA], F32))

        def phase_a_chunk(c):
            b = c % 2
            w = [f"a_tr{c-2}_{ND-1}"] if c >= 2 else []
            rec.op("sync", lambda e, c=c, b=b: e.dma_start(
                out=u_nat[:, b, :, :],
                in_=u_h[c * CA:(c + 1) * CA, :].rearrange(
                    "(a p) d -> p a d", p=128)),
                sig=f"a_u{c}", wait=tuple(w))
            for k in range(ND):
                pb = P[k % 2]
                gk = c * ND + k
                tw = [f"a_u{c}"]
                if gk >= 2:
                    pk = gk - 2
                    tw.append(f"a_uev{pk // ND}_{pk % ND}")
                if c == 0 and k == 0:
                    tw.append(f"bwT_s{2*ND-1}")
                for a in range(4):
                    rec.op("tensor", lambda e, b=b, k=k, a=a, pb=pb:
                           e.transpose(
                               pb[:, a * 128:(a + 1) * 128],
                               u_nat[:, b, a, k * 128:(k + 1) * 128],
                               ident[:]),
                           sig=f"a_tr{c}_{k}" if a == 3 else None,
                           wait=tuple(tw) if a == 0 else ())
                ew = [f"a_tr{c}_{k}"]
                if c >= 2:
                    ew.append(f"a_res{c-2}_{k}")
                rec.op("scalar", lambda e, b=b, k=k, pb=pb: e.copy(
                    uT16[:, b, k, :], pb[:]),
                    sig=f"a_uev{c}_{k}", wait=tuple(ew))
            for nh in range(NH):
                pb = P[2 + nh]
                for k in range(ND):
                    w2 = [f"a_uev{c}_{ND-1}"]
                    if c >= 1 and k == 0:
                        w2.append(f"a_scan{c-1}_{nh}")
                    rec.op("tensor", lambda e, b=b, nh=nh, k=k, pb=pb:
                           e.matmul(
                               pb[:], bwT[:, k, nh * 128:(nh + 1) * 128],
                               uT16[:, b, k, :], start=(k == 0),
                               stop=(k == ND - 1)),
                           sig=f"a_bu{c}_{nh}" if k == ND - 1 else None,
                           wait=tuple(w2) if k == 0 else ())
            for nh in range(NH):
                lamb = s0ap(lam_sb, nh, NH, 128, [[0, CA]])
                init = 0.0 if c == 0 else h32[:, 1 - b, nh, CA - 1:CA]
                w2 = [f"a_bu{c}_{nh}"]
                if c == 0:
                    w2.append("lam")
                if c >= 1:
                    w2.append(f"a_h16{c-1}_{nh}")
                rec.op("vector", lambda e, b=b, nh=nh, lamb=lamb, init=init:
                       e.tensor_tensor_scan(
                           h32[:, b, nh, :], lamb, P[2 + nh][:], init,
                           ALU.mult, ALU.add),
                       sig=f"a_scan{c}_{nh}", wait=tuple(w2))
                w3 = [f"a_scan{c}_{nh}"]
                if c >= 2:
                    w3.append(f"a_y{c-2}_{ND-1}")
                rec.op("vector", lambda e, b=b, nh=nh: e.tensor_copy(
                    h16[:, b, nh, :], h32[:, b, nh, :]),
                    sig=f"a_h16h{c}_{nh}", wait=tuple(w3))
                rec.op("vector", lambda e, b=b, nh=nh: e.tensor_tensor(
                    h16lo[:, b, nh, :], h32[:, b, nh, :], h16[:, b, nh, :],
                    ALU.subtract), sig=f"a_h16{c}_{nh}",
                    wait=(f"a_h16h{c}_{nh}",))
            for k in range(ND):
                pb = P[4 + (k % 2)]
                gk = c * ND + k
                w2 = [f"a_h16{c}_{NH-1}"]
                if gk >= 2:
                    pk = gk - 2
                    w2.append(f"a_res{pk // ND}_{pk % ND}")
                for j, (nh, hsrc) in enumerate(
                        [(n_, h_) for n_ in range(NH)
                         for h_ in (h16, h16lo)]):
                    rec.op("tensor", lambda e, b=b, k=k, nh=nh, hsrc=hsrc,
                           pb=pb, j=j: e.matmul(
                               pb[:], cwT[:, nh, k * 128:(k + 1) * 128],
                               hsrc[:, b, nh, :], start=(j == 0),
                               stop=(j == 2 * NH - 1)),
                           sig=f"a_y{c}_{k}" if j == 2 * NH - 1 else None,
                           wait=tuple(w2) if j == 0 else ())
                w3 = [f"a_y{c}_{k}"]
                if c == 0:
                    w3.append("onepD")
                if c >= 2:
                    w3.append(f"a_ap1_{c-2}_{k}")
                rec.op("vector", lambda e, b=b, k=k, pb=pb:
                       e.scalar_tensor_tensor(
                           x1pre[:, b, k, :], uT16[:, b, k, :],
                           onepD[:, k:k + 1], pb[:], ALU.mult, ALU.add),
                       sig=f"a_res{c}_{k}", wait=tuple(w3))
                w4 = [f"a_res{c}_{k}"]
                if c >= 1 and k == 0:
                    w4.append(f"a_stmm{c-1}")
                rec.op("scalar", lambda e, k=k: e.activation(
                    x1sq[:, k, :], x1pre[:, b, k, :], AF.Square),
                    sig=f"a_sq{c}_{k}", wait=tuple(w4))
            for k in range(ND):
                w2 = []
                if k == 0:
                    w2 = [f"a_res{c}_{ND-1}", f"a_sq{c}_{ND-1}", "ones_c"]
                    if c == 0:
                        w2 += [f"bwT_s{2*ND-1}", f"cwT_s{2*ND-1}"]
                    else:
                        w2 += [f"a_m{c-1}", f"a_ms{c-1}"]
                rec.op("tensor", lambda e, b=b, k=k: e.matmul(
                    P[6][:], ones_mat[:], x1pre[:, b, k, :],
                    start=(k == 0), stop=(k == ND - 1)),
                    wait=tuple(w2))
                rec.op("tensor", lambda e, k=k: e.matmul(
                    P[7][:], ones_mat[:], x1sq[:, k, :],
                    start=(k == 0), stop=(k == ND - 1)),
                    sig=f"a_stmm{c}" if k == ND - 1 else None)
            w2 = [f"a_stmm{c}"]
            if c >= 1:
                w2 += [f"a_ab0_{c-1}", f"a_ab1_{c-1}"]
            rec.op("vector", lambda e: e.tensor_scalar(
                stf[:, 0, :], P[6][:], 1.0 / DM, None, ALU.mult),
                sig=f"a_m{c}", wait=tuple(w2))
            rec.op("vector", lambda e: e.tensor_scalar(
                stf[:, 1, :], P[7][:], 1.0 / DM, None, ALU.mult),
                sig=f"a_ms{c}", wait=tuple(w2))
            rec.op("vector", lambda e: e.tensor_tensor(
                stf[:, 2, :], stf[:, 0, :], stf[:, 0, :], ALU.mult),
                sig=f"a_m2{c}", wait=(f"a_m{c}",))
            rec.op("vector", lambda e: e.tensor_tensor(
                stf[:, 3, :], stf[:, 1, :], stf[:, 2, :], ALU.subtract),
                sig=f"a_var{c}", wait=(f"a_ms{c}", f"a_m2{c}"))
            rec.op("scalar", lambda e: e.activation(
                stf[:, 1, :], stf[:, 3, :], AF.Sqrt,
                bias=eps_sb[:, 0:1], scale=1.0), sig=f"a_sd{c}",
                wait=(f"a_var{c}", "epsz"))
            rec.op("vector", lambda e: e.reciprocal(
                stf[:, 2, :], stf[:, 1, :]), sig=f"a_rstd{c}",
                wait=(f"a_sd{c}",))
            rec.op("vector", lambda e: e.tensor_tensor(
                stf[:, 3, :], stf[:, 0, :], stf[:, 2, :], ALU.mult),
                sig=f"a_mr{c}", wait=(f"a_rstd{c}",))
            w2 = [f"a_rstd{c}"]
            if c >= 2:
                w2.append(f"a_ap1_{c-2}_{ND-1}")
            rec.op("vector", lambda e, b=b: e.tensor_copy(
                ab_b[:, b, 0, :], stf[:, 2, :]), sig=f"a_ab0_{c}",
                wait=tuple(w2))
            w2 = [f"a_mr{c}"]
            if c >= 2:
                w2.append(f"a_ap2_{c-2}_{ND-1}")
            rec.op("vector", lambda e, b=b: e.tensor_copy(
                ab_b[:, b, 1, :], stf[:, 3, :]), sig=f"a_ab1_{c}",
                wait=tuple(w2))
            for k in range(ND):
                w2 = [f"a_ab0_{c}", f"a_ab1_{c}"]
                if k > 0:
                    w2.append(f"a_ap2_{c}_{k-1}")
                if c >= 2 and k == 0:
                    w2 += [f"a_sp{c-2}", f"a_mx{c-2}"]
                rec.op("vector", lambda e, b=b, k=k: e.tensor_tensor(
                    tmpa[:, b, :], x1pre[:, b, k, :], ab_b[:, b, 0, :],
                    ALU.mult), sig=f"a_ap1_{c}_{k}", wait=tuple(w2))
                rec.op("vector", lambda e, b=b, k=k: e.tensor_tensor(
                    x1out[:, b, k, :], tmpa[:, b, :], ab_b[:, b, 1, :],
                    ALU.subtract), sig=f"a_ap2_{c}_{k}",
                    wait=(f"a_ap1_{c}_{k}",))
            full = x1out.ap().rearrange("p a k t -> p a (k t)")
            W = ND * CA
            w2 = [f"a_ap2_{c}_{ND-1}", "zmx"]
            if c >= 1:
                w2.append(f"a_mx{c-1}")
            rec.op("vector", lambda e, b=b, full=full, W=W: e.tensor_tensor(
                runmx[:, 0:W], full[:, b], runmx[:, 0:W], ALU.max),
                sig=f"a_mxp{c}", wait=tuple(w2))
            rec.op("vector", lambda e, b=b, full=full, W=W: e.tensor_tensor(
                runmn[:, 0:W], full[:, b], runmn[:, 0:W], ALU.min),
                sig=f"a_mx{c}", wait=(f"a_mxp{c}", "zmn"))
            rec.op("gdma", lambda e, c=c, b=b: e.dma_start(
                out=x1s.ap().rearrange("(k p) T -> p k T", p=128)[
                    :, :, c * CA:(c + 1) * CA],
                in_=x1out[:, b, :, :]),
                sig=f"a_sp{c}", wait=(f"a_ap2_{c}_{ND-1}",))

        for c in range(NCA):
            phase_a_chunk(c)
            if c < 9:
                for eng, fn, sig, w in dequant_ops[c]:
                    rec.op(eng, fn, sig=sig, wait=w)

        rec.op("gpsimd", lambda e: e.collective_compute(
            "AllGather", ALU.bypass, replica_groups=[list(range(8))],
            ins=[wag_in.ap().opt()], outs=[wag.ap().opt()]),
            sig="w_ag", wait=tuple(f"wq_sp{s}" for s in range(9)))
        if "wagdbg" in dbg:
            rec.op("gdma", lambda e: e.dma_start(
                out=wag_dbg.ap(), in_=wag.ap()), sig="w_agdbg",
                wait=("w_ag",))


        def fold_runs(pre, W, rmax, wait_ev):
            rec.op("vector", lambda e, W=W, rmax=rmax: e.tensor_reduce(
                rmax[:, 0:1], runmx[:, 0:W], AX.X, ALU.max),
                sig=f"{pre}_fmx", wait=wait_ev)
            rec.op("vector", lambda e, W=W, rmax=rmax: e.tensor_reduce(
                rmax[:, 1:2], runmn[:, 0:W], AX.X, ALU.min, negate=True),
                sig=f"{pre}_fmn", wait=(f"{pre}_fmx",))
            rec.op("vector", lambda e, W=W: e.memset(runmx[:, 0:W], -1e30),
                sig=f"{pre}_zx", wait=(f"{pre}_fmx",))
            rec.op("vector", lambda e, W=W: e.memset(runmn[:, 0:W], 1e30),
                sig=f"{pre}_zn", wait=(f"{pre}_fmn",))

        def sc(i):
            return sc_sb[0:1, i:i + 1]

        def local_gmax(pre, rmax, lane, wait_ev, tr_wait):
            rec.op("vector", lambda e, rmax=rmax: e.tensor_reduce(
                redc[:], rmax[:], AX.X, ALU.max), sig=f"{pre}_red",
                wait=wait_ev)
            rec.op("vector", lambda e: e.tensor_copy(
                redm[:, 0:1], redc[:]), sig=f"{pre}_rm",
                wait=(f"{pre}_red", "redmz"))
            rec.op("tensor", lambda e: e.transpose(
                P[6][:, 0:128], redm[:], ident[:]), sig=f"{pre}_tr",
                wait=(f"{pre}_rm",) + tuple(tr_wait))
            rec.op("vector", lambda e: e.tensor_copy(
                gm2[:], P[6][:, 0:128]), sig=f"{pre}_ev",
                wait=(f"{pre}_tr",))
            rec.op("vector", lambda e, lane=lane: e.tensor_reduce(
                cc_sb[0:1, lane:lane + 1], gm2[0:1, :], AX.X, ALU.max),
                sig=f"{pre}_m", wait=(f"{pre}_ev", "ccz"))

        def do_cc(pre, idx, wait_ev):
            cin, cout = cc_bufs[idx]
            rec.op("gdma", lambda e, cin=cin: e.dma_start(
                out=cin.ap(), in_=cc_sb[:]), sig=f"{pre}_di", wait=wait_ev)
            rec.op("gpsimd", lambda e, cin=cin, cout=cout:
                   e.collective_compute(
                       "AllReduce", ALU.max, replica_groups=[list(range(8))],
                       ins=[cin.ap().opt()], outs=[cout.ap().opt()]),
                   sig=f"{pre}_cc", wait=(f"{pre}_di",))
            rec.op("gdma", lambda e, cout=cout: e.dma_start(
                out=ccr_sb[:], in_=cout.ap()), sig=f"{pre}_do",
                wait=(f"{pre}_cc",))

        def finish(pre, pp):
            rec.op("tensor", lambda e: e.matmul(
                P[6][:, 0:8], onesf[:], sc_sb[0:1, 0:8], start=True,
                stop=True), sig=f"{pre}_fmm",
                wait=(f"{pre}_sc", f"{pre}_ev", "onesf"))
            rec.op("vector", lambda e, pp=pp: e.tensor_copy(
                pp[:, 0:8], P[6][:, 0:8]), sig=f"{pre}_fin",
                wait=(f"{pre}_fmm",))

        fold_runs("ar0f", ND * CA, rmax_x1, (f"a_mx{NCA-1}",))
        local_gmax("ar0", rmax_x1, 0, ("ar0f_fmn",),
                   (f"a_m{NCA-1}", f"a_ms{NCA-1}"))
        do_cc("ar0", 0, (f"ar0_m",))
        rec.op("vector", lambda e: e.tensor_scalar(
            sc(3), ccr_sb[0:1, 0:1], 1.0 / QMAX, 1e-12, ALU.mult, ALU.max),
            sig="ar0_s1", wait=("ar0_do",))
        rec.op("vector", lambda e: e.reciprocal(sc(0), sc(3)),
               sig="ar0_sc", wait=("ar0_s1",))
        finish("ar0", pp1)
        esA.close()
        if max_phase < 1:
            rec.op("sync", lambda e: None, wait=("ar0_fin", f"a_sp{NCA-1}"))
            with nc.Block() as block:
                rec.emit(block, sems, prefix=prefix)
            return nc
        BAR_A = ("ar0_tr", "ar0_sc", f"a_sd{NCA-1}", "ar0_do", "ar0_fin",
                 f"a_u{NCA-1}", f"wq_ld{8}")

        esBc = ExitStack()
        ebc = esBc.enter_context
        x1r = ebc(nc.sbuf_tensor("x1r", [128, ND, CA], F16))
        x1u = ebc(nc.sbuf_tensor("x1u", [128, ND, CA], U16))
        x1i = ebc(nc.sbuf_tensor("x1i", [128, 2, ND, CA], F16))
        ev_ring = ebc(nc.sbuf_tensor("ev_ring", [128, 8, CA], F16))
        wuT = ebc(nc.sbuf_tensor("wuT", [128, ND, DI], F16))
        esB1 = ExitStack()
        wgT = esB1.enter_context(nc.sbuf_tensor("wgT", [128, ND, DI], F16))

        for k in range(ND):
            rec.op("synct", lambda e, k=k: e.dma_start_transpose(
                wgT[:, k, :], wag.ap()[24 * 128:48 * 128,
                                       k * 128:(k + 1) * 128]),
                sig=f"b_wg{k}", wait=("w_ag",) + BAR_A)
        for k in range(ND):
            rec.op("synct", lambda e, k=k: e.dma_start_transpose(
                wuT[:, k, :], wag.ap()[0:24 * 128, k * 128:(k + 1) * 128]),
                sig=f"b_wu{k}", wait=("w_ag",) + BAR_A)

        def b_prep_chunk(ph, c, wname):
            b = c % 2
            w = [f"a_sp{c}", "ar0_fin"]
            if ph == "b1" and c == 0:
                w += list(BAR_A)
            if ph == "b2" and c == 0:
                w.append(f"b1_mm{NCA * NI - 1}")
            if c >= 1:
                w.append(f"{ph}_x1u{c-1}")
            rec.op("sync", lambda e, c=c: e.dma_start(
                out=x1r[:, :, :],
                in_=x1s.ap().rearrange("(k p) T -> p k T", p=128)[
                    :, :, c * CA:(c + 1) * CA]),
                sig=f"{ph}_x1{c}", wait=tuple(w))
            fl = x1r.ap().rearrange("p k t -> p (k t)")
            fu = x1u.ap().rearrange("p k t -> p (k t)")
            fi = x1i.ap().rearrange("p a k t -> p a (k t)")
            w2 = [f"{ph}_x1{c}"]
            if c >= 1:
                w2.append(f"{ph}_x1i{c-1}")
            rec.op("vector", lambda e, fl=fl, fu=fu: e.tensor_scalar(
                fu, fl, pp1[:, 0:1], ROFF, ALU.mult, ALU.add),
                sig=f"{ph}_x1u{c}", wait=tuple(w2))
            w2 = [f"{ph}_x1u{c}"]
            if c >= 2:
                w2.append(f"{ph}_mm{(c-1)*NI-1}")
            rec.op("vector", lambda e, b=b, fu=fu, fi=fi: e.tensor_scalar(
                fi[:, b], fu, -SHIFT, None, ALU.add),
                sig=f"{ph}_x1i{c}", wait=tuple(w2))

        def b_phase_chunk(ph, c, wT, dst, wname):
            b = c % 2
            for it in range(NI):
                g = c * NI + it
                pb = P[g % 4]
                wt = [f"{ph}_x1i{c}",
                      f"b_wg{ND-1}" if ph == "b1" else f"b_wu{ND-1}"]
                if g >= 4:
                    wt.append(f"{ph}_ev{g-4}")
                for k in range(ND):
                    rec.op("tensor",
                           lambda e, b=b, wT=wT, it=it, k=k, pb=pb:
                           e.matmul(
                               pb[:], wT[:, k, it * 128:(it + 1) * 128],
                               x1i[:, b, k, :], start=(k == 0),
                               stop=(k == ND - 1)),
                           sig=f"{ph}_mm{g}" if k == ND - 1 else None,
                           wait=tuple(wt) if k == 0 else ())
                slot = g % 8
                wt2 = [f"{ph}_mm{g}"]
                if g >= 8:
                    j = (g - 8) // 8
                    wt2 += [f"{ph}_sp{j}", f"{ph}_mx{j}"]
                rec.op("scalar", lambda e, slot=slot, pb=pb:
                       e.activation(ev_ring[:, slot, :], pb[:], AF.Copy,
                                    bias=0.0, scale=0.125),
                       sig=f"{ph}_ev{g}", wait=tuple(wt2))
            for sp in range(3):
                g0 = c * NI + sp * 8
                j = g0 // 8
                it0 = sp * 8
                s0 = g0 % 8
                rec.op("gdma", lambda e, c=c, dst=dst, it0=it0, s0=s0:
                       e.dma_start(
                           out=dst.ap().rearrange(
                               "(i p) T -> p i T", p=128)[
                               :, it0:it0 + 8, c * CA:(c + 1) * CA],
                           in_=ev_ring[:, s0:s0 + 8, :]),
                       sig=f"{ph}_sp{j}", wait=(f"{ph}_ev{g0+7}",))
                w2 = [f"{ph}_ev{g0+7}", "ar0f_zx", "ar0f_zn"]
                if j >= 1:
                    w2.append(f"{ph}_mx{j-1}")
                rall = ev_ring.ap().rearrange("p s t -> p (s t)")[
                    :, s0 * CA:(s0 + 8) * CA]
                rec.op("vector", lambda e, rall=rall: e.tensor_tensor(
                    runmx[:, 0:8 * CA], rall, runmx[:, 0:8 * CA], ALU.max),
                    sig=f"{ph}_mxa{j}", wait=tuple(w2))
                rec.op("vector", lambda e, rall=rall: e.tensor_tensor(
                    runmn[:, 0:8 * CA], rall, runmn[:, 0:8 * CA], ALU.min),
                    sig=f"{ph}_mx{j}", wait=(f"{ph}_mxa{j}",))

        b_prep_chunk("b1", 0, "g")
        for c in range(NCA):
            if c + 1 < NCA:
                b_prep_chunk("b1", c + 1, "g")
            b_phase_chunk("b1", c, wgT, gis, "g")
        rec.op("vector", lambda e: e.tensor_reduce(
            rmax_gi[:, 0:1], runmx[:, 0:8 * CA], AX.X, ALU.max),
            sig="b1_fmx", wait=(f"b1_mx{NCA*3-1}",))
        rec.op("vector", lambda e: e.tensor_reduce(
            rmax_gi[:, 1:2], runmn[:, 0:8 * CA], AX.X, ALU.min, negate=True),
            sig="b1_fmn", wait=("b1_fmx",))
        rec.op("vector", lambda e: e.memset(runmx[:, 0:8 * CA], -1e30),
               sig="b1_zx", wait=("b1_fmx",))
        rec.op("vector", lambda e: e.memset(runmn[:, 0:8 * CA], 1e30),
               sig="b1_zn", wait=("b1_fmn",))
        local_gmax("arg", rmax_gi, 1, ("b1_fmn", "ar0_m", "ar0_di"),
                   ("ar0_fin",))
        do_cc("arg", 1, ("arg_m",))
        rec.op("vector", lambda e: e.tensor_tensor(
            sc(9), ccr_sb[0:1, 1:2], pp1[0:1, 3:4], ALU.mult),
            sig="arg_t3", wait=("arg_do", "ar0_fin"))
        rec.op("vector", lambda e: e.tensor_scalar(
            sc(9), sc(9), 8.0 / QMAX, 1e-12, ALU.mult, ALU.max),
            sig="arg_t4", wait=("arg_t3",))
        rec.op("vector", lambda e: e.reciprocal(sc(11), sc(9)),
               sig="arg_t6", wait=("arg_t4",))
        rec.op("vector", lambda e: e.tensor_tensor(
            sc(0), pp1[0:1, 3:4], sc(11), ALU.mult),
            sig="arg_t8a", wait=("arg_t6",))
        rec.op("vector", lambda e: e.tensor_scalar(
            sc(0), sc(0), 8.0, None, ALU.mult),
            sig="arg_t8", wait=("arg_t8a",))
        rec.op("vector", lambda e: e.tensor_copy(sc(1), sc(9)),
               sig="arg_t9", wait=("arg_t4",))
        rec.op("vector", lambda e: e.tensor_scalar(
            sc(2), sc(9), -SHIFT, None, ALU.mult),
            sig="arg_tA", wait=("arg_t4",))
        rec.op("vector", lambda e: e.tensor_copy(sc(5), sc(9)),
            sig="arg_sc", wait=("arg_tA", "arg_t8", "arg_t9"))
        finish("arg", pp2)
        esB1.close()

        esC1 = ExitStack()
        ecc = esC1.enter_context
        gi_r = ecc(nc.sbuf_tensor("gi_r", [128, NI, CC], F16))
        gi_u = ecc(nc.sbuf_tensor("gi_u", [128, NI, CC], U16))
        sg_t = ecc(nc.sbuf_tensor("sg_t", [128, NI, CC], F16))
        sigi = ecc(nc.sbuf_tensor("sigi", [128, NI, CC], U16))
        gi_s = ecc(nc.sbuf_tensor("gi_s", [128, NI, CC], F16))
        g2_t = ecc(nc.sbuf_tensor("g2_t", [128, NI, 2 * CC], F16))

        C1W = 1536

        def c1_chunk(c):
            half = c % 2
            w = ["arg_fin", f"b1_sp{(c // 2) * 3 + 0}",
                 f"b1_sp{(c // 2) * 3 + 1}", f"b1_sp{(c // 2) * 3 + 2}"]
            if c >= 1:
                w.append(f"c1_u{c-1}")
            rec.op("sync", lambda e, c=c: e.dma_start(
                out=gi_r[:, :, :],
                in_=gis.ap().rearrange("(i p) T -> p i T", p=128)[
                    :, :, c * CC:(c + 1) * CC]),
                sig=f"c1_ld{c}", wait=tuple(w))
            flu = gi_u.ap().rearrange("p i t -> p (i t)")
            fls = sg_t.ap().rearrange("p i t -> p (i t)")
            fli = sigi.ap().rearrange("p i t -> p (i t)")
            fgs = gi_s.ap().rearrange("p i t -> p (i t)")
            flr = gi_r.ap().rearrange("p i t -> p (i t)")
            w2 = [f"c1_ld{c}"]
            if c >= 1:
                w2.append(f"c1_g2p{c-1}")
            rec.op("vector", lambda e, flu=flu, flr=flr: e.tensor_scalar(
                flu, flr, pp2[:, 0:1], ROFF, ALU.mult, ALU.add),
                sig=f"c1_u{c}", wait=tuple(w2))
            w2 = [f"c1_u{c}"]
            if c >= 1:
                w2.append(f"c1_si{c-1}")
            rec.op("scalar", lambda e, flu=flu, fls=fls: e.activation(
                fls, flu, AF.Sigmoid, bias=pp2[:, 2:3], scale=pp2[:, 1:2]),
                sig=f"c1_sg{c}", wait=tuple(w2))
            w2 = [f"c1_sg{c}"]
            if c >= 1:
                w2.append(f"c1_g2p{c-1}")
            rec.op("scalar", lambda e, fls=fls, fli=fli: e.activation(
                fli, fls, AF.Copy, bias=0.0, scale=65536.0),
                sig=f"c1_si{c}", wait=tuple(w2))
            w2 = [f"c1_u{c}"]
            if c >= 1:
                w2.append(f"c1_g2p{c-1}")
            rec.op("vector", lambda e, fgs=fgs, flu=flu: e.tensor_scalar(
                fgs, flu, -SHIFT, 2.0 ** -16, ALU.add, ALU.mult),
                sig=f"c1_gs{c}", wait=tuple(w2))
            w2 = [f"c1_si{c}", f"c1_gs{c}"]
            if c >= 2:
                w2 += [f"c1_g2{(c // 2 - 1) * 2}", f"c1_mx{c-2}"]
            rec.op("vector", lambda e, half=half:
                   e.tensor_tensor(
                       g2_t[:, :, half * CC:(half + 1) * CC],
                       gi_s[:, :, :], sigi[:, :, :], ALU.mult),
                   sig=f"c1_g2p{c}", wait=tuple(w2))
            w2 = [f"c1_g2p{c}", "b1_zx", "b1_zn"]
            if c >= 1:
                w2.append(f"c1_mx{c-1}")
            base = 8 * CA
            for q in range(4):
                s_ap = g2_t[:, q * 6:(q + 1) * 6,
                            half * CC:(half + 1) * CC]
                rmxs = runmx[:, base:base + C1W].rearrange(
                    "p (i t) -> p i t", t=CC)
                rmns = runmn[:, base:base + C1W].rearrange(
                    "p (i t) -> p i t", t=CC)
                rec.op("vector", lambda e, s_ap=s_ap, rmxs=rmxs:
                       e.tensor_tensor(rmxs, s_ap, rmxs, ALU.max),
                       sig=f"c1_mxa{c}_{q}" if q < 3 else f"c1_mxa{c}",
                       wait=tuple(w2) if q == 0 else
                       (f"c1_mxa{c}_{q-1}",))
            last = f"c1_mxa{c}"
            for q in range(4):
                s_ap = g2_t[:, q * 6:(q + 1) * 6,
                            half * CC:(half + 1) * CC]
                rmns = runmn[:, base:base + C1W].rearrange(
                    "p (i t) -> p i t", t=CC)
                rec.op("vector", lambda e, s_ap=s_ap, rmns=rmns:
                       e.tensor_tensor(rmns, s_ap, rmns, ALU.min),
                       sig=f"c1_mxn{c}_{q}" if q < 3 else f"c1_mx{c}",
                       wait=(last,) if q == 0 else (f"c1_mxn{c}_{q-1}",))
            if half == 1:
                rec.op("gdma", lambda e, c=c: e.dma_start(
                    out=g2s.ap().rearrange("(i p) T -> p i T", p=128)[
                        :, :, (c - 1) * CC:(c + 1) * CC],
                    in_=g2_t[:, :, :]),
                    sig=f"c1_g2{c-1}", wait=(f"c1_g2p{c}", f"c1_mx{c}"))

        b_prep_chunk("b2", 0, "u")
        for c in range(NCA):
            if c + 1 < NCA:
                b_prep_chunk("b2", c + 1, "u")
            b_phase_chunk("b2", c, wuT, uis, "u")
            c1_chunk(2 * c)
            c1_chunk(2 * c + 1)

        rec.op("vector", lambda e: e.tensor_reduce(
            rmax_ui[:, 0:1], runmx[:, 0:8 * CA], AX.X, ALU.max),
            sig="b2_fmx", wait=(f"b2_mx{NCA*3-1}",))
        rec.op("vector", lambda e: e.tensor_reduce(
            rmax_ui[:, 1:2], runmn[:, 0:8 * CA], AX.X, ALU.min, negate=True),
            sig="b2_fmn", wait=("b2_fmx",))
        rec.op("vector", lambda e: e.tensor_reduce(
            rmax_g2[:, 0:1], runmx[:, 8 * CA:8 * CA + C1W], AX.X, ALU.max),
            sig="c1_fmx", wait=(f"c1_mx{2*NCA-1}",))
        rec.op("vector", lambda e: e.tensor_reduce(
            rmax_g2[:, 1:2], runmn[:, 8 * CA:8 * CA + C1W], AX.X, ALU.min,
            negate=True), sig="c1_fmn", wait=("c1_fmx",))
        rec.op("vector", lambda e: e.memset(runmx[:], -1e30), sig="b2_zx",
               wait=("b2_fmx", "c1_fmx"))
        rec.op("vector", lambda e: e.memset(runmn[:], 1e30), sig="b2_zn",
               wait=("b2_fmn", "c1_fmn"))

        local_gmax("aru", rmax_ui, 2, ("b2_fmn", "arg_m", "arg_di"),
                   ("arg_fin",))
        local_gmax("ar2", rmax_g2, 3, ("aru_m", "c1_fmn"), ("aru_ev",))
        do_cc("ar2", 2, ("aru_m", "ar2_m"))
        rec.op("vector", lambda e: e.tensor_tensor(
            sc(8), ccr_sb[0:1, 2:3], pp1[0:1, 3:4], ALU.mult),
            sig="ar2_t1", wait=("ar2_do", "ar0_fin", "arg_fin"))
        rec.op("vector", lambda e: e.tensor_scalar(
            sc(8), sc(8), 8.0 / QMAX, 1e-12, ALU.mult, ALU.max),
            sig="ar2_t2", wait=("ar2_t1",))
        rec.op("vector", lambda e: e.reciprocal(sc(10), sc(8)),
               sig="ar2_t5", wait=("ar2_t2",))
        rec.op("vector", lambda e: e.tensor_tensor(
            sc(4), pp1[0:1, 3:4], sc(10), ALU.mult),
            sig="ar2_t7a", wait=("ar2_t5",))
        rec.op("vector", lambda e: e.tensor_scalar(
            sc(4), sc(4), 8.0, None, ALU.mult),
            sig="ar2_t7", wait=("ar2_t7a",))
        rec.op("vector", lambda e: e.tensor_copy(sc(6), sc(8)),
               sig="ar2_t7s", wait=("ar2_t2",))
        rec.op("vector", lambda e: e.tensor_tensor(
            sc(11), ccr_sb[0:1, 3:4], pp2[0:1, 5:6], ALU.mult),
            sig="ar2_t8", wait=("ar2_do", "arg_fin"))
        rec.op("vector", lambda e: e.tensor_scalar(
            sc(11), sc(11), 1.0 / QMAX, 1e-12, ALU.mult, ALU.max),
            sig="ar2_t9", wait=("ar2_t8",))
        rec.op("vector", lambda e: e.tensor_copy(sc(3), sc(11)),
               sig="ar2_tA", wait=("ar2_t9",))
        rec.op("vector", lambda e: e.reciprocal(sc(12), sc(11)),
               sig="ar2_tB", wait=("ar2_t9",))
        rec.op("vector", lambda e: e.tensor_tensor(
            sc(0), pp2[0:1, 5:6], sc(12), ALU.mult),
            sig="ar2_sc", wait=("ar2_tB", "ar2_t7", "ar2_t7s", "ar2_tA"))
        finish("ar2", pp3)
        esC1.close()
        esBc.close()
        BAR_C1 = ("ar2_tr", "ar2_sc", f"c1_sg{2*NCA-1}", "ar2_do",
                  "ar2_fin")

        esC2 = ExitStack()
        ec2 = esC2.enter_context
        ui_r = ec2(nc.sbuf_tensor("ui_r", [128, 2, NI, 2 * CC], F16))
        g2_r = ec2(nc.sbuf_tensor("g2_r", [128, 2, NI, 2 * CC], F16))
        ui_u = ec2(nc.sbuf_tensor("ui_u", [128, NI, CC], U16))
        ui_s = ec2(nc.sbuf_tensor("ui_s", [128, NI, CC], F16))
        g2i_u = ec2(nc.sbuf_tensor("g2i_u", [128, NI, CC], U16))
        g2i_s = ec2(nc.sbuf_tensor("g2i_s", [128, NI, CC], F16))
        op_t = ec2(nc.sbuf_tensor("op_t", [128, NI, 2 * CC], F16))

        def c2_chunk(c):
            lb = (c // 2) % 2
            half = c % 2
            if half == 0:
                w = ["ar2_fin", f"b2_sp{(c // 2) * 3 + 0}",
                     f"b2_sp{(c // 2) * 3 + 1}",
                     f"b2_sp{(c // 2) * 3 + 2}"]
                if c == 0:
                    w += list(BAR_C1)
                if c >= 4:
                    w.append(f"c2_uu{c-3}")
                rec.op("sync", lambda e, c=c, lb=lb: e.dma_start(
                    out=ui_r[:, lb, :, :],
                    in_=uis.ap().rearrange("(i p) T -> p i T", p=128)[
                        :, :, c * CC:(c + 2) * CC]),
                    sig=f"c2_lu{c}", wait=tuple(w))
                w = ["ar2_fin", f"c1_g2{c}"]
                if c == 0:
                    w += list(BAR_C1)
                if c >= 4:
                    w.append(f"c2_gu{c-3}")
                rec.op("sync", lambda e, c=c, lb=lb: e.dma_start(
                    out=g2_r[:, lb, :, :],
                    in_=g2s.ap().rearrange("(i p) T -> p i T", p=128)[
                        :, :, c * CC:(c + 2) * CC]),
                    sig=f"c2_lg{c}", wait=tuple(w))
            flu = ui_u.ap().rearrange("p i t -> p (i t)")
            fls = ui_s.ap().rearrange("p i t -> p (i t)")
            flg = g2i_u.ap().rearrange("p i t -> p (i t)")
            fgi = g2i_s.ap().rearrange("p i t -> p (i t)")
            w2 = [f"c2_lu{c - half}"]
            if c >= 1:
                w2.append(f"c2_us{c-1}")
            rec.op("scalar", lambda e, lb=lb, half=half, flu=flu:
                   e.activation(
                       flu, ui_r[:, lb, :, half * CC:(half + 1) * CC],
                       AF.Copy, bias=ROFF, scale=pp3[:, 4:5]),
                   sig=f"c2_uu{c}", wait=tuple(w2))
            w2 = [f"c2_uu{c}"]
            if c >= 1:
                w2.append(f"c2_op{c-1}")
            rec.op("vector", lambda e, flu=flu, fls=fls: e.tensor_scalar(
                fls, flu, -SHIFT, 2.0 ** -13, ALU.add, ALU.mult),
                sig=f"c2_us{c}", wait=tuple(w2))
            w2 = [f"c2_lg{c - half}"]
            if c >= 1:
                w2.append(f"c2_gs{c-1}")
            rec.op("scalar", lambda e, lb=lb, half=half, flg=flg:
                   e.activation(
                       flg, g2_r[:, lb, :, half * CC:(half + 1) * CC],
                       AF.Copy, bias=ROFF, scale=pp3[:, 0:1]),
                   sig=f"c2_gu{c}", wait=tuple(w2))
            w2 = [f"c2_gu{c}"]
            if c >= 1:
                w2.append(f"c2_op{c-1}")
            rec.op("vector", lambda e, fgi=fgi, flg=flg: e.tensor_scalar(
                fgi, flg, -SHIFT, 2.0 ** -16, ALU.add, ALU.mult),
                sig=f"c2_gs{c}", wait=tuple(w2))
            w2 = [f"c2_gs{c}", f"c2_us{c}"]
            if c >= 2:
                w2 += [f"c2_sp{(c // 2 - 1) * 2}", f"c2_mx{c-2}"]
            rec.op("vector", lambda e, half=half:
                   e.tensor_tensor(
                       op_t[:, :, half * CC:(half + 1) * CC],
                       g2i_s[:, :, :], ui_s[:, :, :], ALU.mult),
                   sig=f"c2_op{c}", wait=tuple(w2))
            W2 = NI * CC // 2
            w2 = [f"c2_op{c}", "b2_zx", "b2_zn"]
            if c >= 1:
                w2.append(f"c2_mx{c-1}")
            rmxs = runmx[:, 0:W2].rearrange("p (i t) -> p i t", t=CC)
            rmns = runmn[:, 0:W2].rearrange("p (i t) -> p i t", t=CC)
            for q in range(2):
                s_ap = op_t[:, q * (NI // 2):(q + 1) * (NI // 2),
                            half * CC:(half + 1) * CC]
                rec.op("vector", lambda e, s_ap=s_ap, rmxs=rmxs:
                       e.tensor_tensor(rmxs, s_ap, rmxs, ALU.max),
                       sig=f"c2_mxa{c}_{q}",
                       wait=tuple(w2) if q == 0 else (f"c2_mxa{c}_0",))
            for q in range(2):
                s_ap = op_t[:, q * (NI // 2):(q + 1) * (NI // 2),
                            half * CC:(half + 1) * CC]
                rec.op("vector", lambda e, s_ap=s_ap, rmns=rmns:
                       e.tensor_tensor(rmns, s_ap, rmns, ALU.min),
                       sig=f"c2_mxn{c}_{q}" if q == 0 else f"c2_mx{c}",
                       wait=(f"c2_mxa{c}_1",) if q == 0
                       else (f"c2_mxn{c}_0",))
            if half == 1:
                rec.op("gdma", lambda e, c=c: e.dma_start(
                    out=ops_.ap().rearrange("(i p) T -> p i T", p=128)[
                        :, :, (c - 1) * CC:(c + 1) * CC],
                    in_=op_t[:, :, :]),
                    sig=f"c2_sp{c-1}", wait=(f"c2_op{c}", f"c2_mx{c}"))

        for c in range(NCC):
            c2_chunk(c)

        fold_runs("ar3f", NI * CC // 2, rmax_o, (f"c2_mx{NCC-1}",))
        local_gmax("ar3", rmax_o, 4, ("ar3f_fmn", "ar2_m", "ar2_di"),
                   ("ar2_fin",))
        do_cc("ar3", 3, ("ar3_m",))
        rec.op("vector", lambda e: e.tensor_tensor(
            sc(9), pp3[0:1, 6:7], pp3[0:1, 3:4], ALU.mult),
            sig="ar3_t2", wait=("ar3_do", "ar2_fin"))
        rec.op("vector", lambda e: e.tensor_tensor(
            sc(10), ccr_sb[0:1, 4:5], sc(9), ALU.mult),
            sig="ar3_t3", wait=("ar3_t2",))
        rec.op("vector", lambda e: e.tensor_scalar(
            sc(10), sc(10), (2.0 ** 29) / QMAX, 1e-12, ALU.mult, ALU.max),
            sig="ar3_t4", wait=("ar3_t3",))
        rec.op("vector", lambda e: e.reciprocal(sc(11), sc(10)),
               sig="ar3_t5", wait=("ar3_t4",))
        rec.op("vector", lambda e: e.tensor_tensor(
            sc(0), sc(9), sc(11), ALU.mult),
            sig="ar3_t6a", wait=("ar3_t5",))
        rec.op("vector", lambda e: e.tensor_scalar(
            sc(0), sc(0), 2.0 ** 29, None, ALU.mult),
            sig="ar3_t6", wait=("ar3_t6a",))
        rec.op("vector", lambda e: e.tensor_copy(sc(3), sc(10)),
               sig="ar3_sc", wait=("ar3_t6",))
        finish("ar3", pp4)
        esC2.close()
        esRun.close()
        if max_phase < 4:
            rec.op("sync", lambda e: None, wait=("ar3_fin",))
            with nc.Block() as block:
                rec.emit(block, sems)
            return nc
        BAR_C2 = ("ar3_tr", "ar3_sc", f"c1_sg{NCC-1}", "ar3_do", "ar3_fin")

        esD = ExitStack()
        ed = esD.enter_context
        wdT = ed(nc.sbuf_tensor("wdT", [128, NI, DM], F16))
        o_r = ed(nc.sbuf_tensor("o_r", [128, 2, NI, 2 * CC], F16))
        oi_u = ed(nc.sbuf_tensor("oi_u", [128, NI, CC], U16))
        oi_b = ed(nc.sbuf_tensor("oi_b", [128, 2, NI, CC], F16))
        x1T = ed(nc.sbuf_tensor("x1T", [128, 4, 2, DM], F16))
        x2_t = ed(nc.sbuf_tensor("x2_t", [128, 2, 2, DM], F16))
        o_sb = ed(nc.sbuf_tensor("o_sb", [128, 2, 2, DM], F32))
        dcol = ed(nc.sbuf_tensor("dcol", [128, 8], F32))
        dsq = ed(nc.sbuf_tensor("dsq", [128, 2, DM], F16))

        for it in range(NI):
            s = it // ND
            q = it % ND
            rec.op("synct", lambda e, s=s, q=q, it=it: e.dma_start_transpose(
                wdT[:, it, :],
                wag.ap()[(48 + 8 * s) * 128:(48 + 8 * s + 8) * 128,
                         q * 128:(q + 1) * 128]),
                sig=f"d_wd{it}", wait=("w_ag",) + BAR_C2)

        def d_chunk(c):
            b = c % 2
            lb = (c // 2) % 2
            half = c % 2
            if half == 0:
                w = ["ar3_fin", f"c2_sp{c}"]
                if c == 0:
                    w += list(BAR_C2)
                if c >= 4:
                    w.append(f"d_ou{c-3}")
                rec.op("sync", lambda e, c=c, lb=lb: e.dma_start(
                    out=o_r[:, lb, :, :],
                    in_=ops_.ap().rearrange("(i p) T -> p i T", p=128)[
                        :, :, c * CC:(c + 2) * CC]),
                    sig=f"d_lo{c}", wait=tuple(w))
            xb = c % 4
            for tt in range(2):
                wx = ["ar3_fin", f"a_sp{(2 * c + tt) // 4}"]
                if c >= 4:
                    wx.append(f"d_ev{((c-4)*2+tt)*2+1}")
                rec.op("synct",
                       lambda e, c=c, xb=xb, tt=tt:
                       e.dma_start_transpose(
                           x1T[:, xb, tt, :],
                           x1s.ap()[:, (2 * c + tt) * 128:
                                    (2 * c + tt + 1) * 128]),
                       sig=f"d_x1{c}_{tt}", wait=tuple(wx))
            flu = oi_u.ap().rearrange("p i t -> p (i t)")
            flb = oi_b.ap().rearrange("p a i t -> p a (i t)")
            w2 = [f"d_lo{c - half}"]
            if c >= 1:
                w2.append(f"d_oib{c-1}")
            rec.op("vector", lambda e, lb=lb, half=half, flu=flu:
                   e.tensor_scalar(
                       flu, o_r[:, lb, :, half * CC:(half + 1) * CC],
                       pp4[:, 0:1], ROFF, ALU.mult, ALU.add),
                   sig=f"d_ou{c}", wait=tuple(w2))
            w2 = [f"d_ou{c}"]
            if c >= 2:
                w2.append(f"d_mm{((c-2)*2+1)*2+1}")
            rec.op("vector", lambda e, b=b, flb=flb, flu=flu:
                   e.tensor_scalar(
                       flb[:, b], flu, -SHIFT, None, ALU.add),
                   sig=f"d_oib{c}", wait=tuple(w2))
            for tt in range(2):
                for dh in range(2):
                    g = (c * 2 + tt) * 2 + dh
                    pb = P[g % 4]
                    wt = [f"d_oib{c}", f"d_wd{NI-1}"]
                    if g >= 4:
                        wt.append(f"d_ev{g-4}")
                    for it in range(NI):
                        rec.op("tensor",
                               lambda e, b=b, tt=tt, dh=dh, it=it, pb=pb:
                               e.matmul(
                                   pb[:],
                                   oi_b[:, b, it, tt * 128:(tt + 1) * 128],
                                   wdT[:, it, dh * 512:(dh + 1) * 512],
                                   start=(it == 0), stop=(it == NI - 1)),
                               sig=f"d_mm{g}" if it == NI - 1 else None,
                               wait=tuple(wt) if it == 0 else ())
                    w2 = [f"d_mm{g}", f"d_x1{c}_{tt}"]
                    if c >= 2:
                        w2.append(f"d_ap{c-2}_{tt}")
                    rec.op("vector", lambda e, b=b, xb=xb, tt=tt, dh=dh,
                           pb=pb: e.scalar_tensor_tensor(
                               x2_t[:, b, tt, dh * 512:(dh + 1) * 512],
                               pb[:], pp4[:, 3:4],
                               x1T[:, xb, tt, dh * 512:(dh + 1) * 512],
                               ALU.mult, ALU.add),
                           sig=f"d_ev{g}", wait=tuple(w2))
                gl = (c * 2 + tt) * 2 + 1
                p_ = c * 2 + tt
                prev_ap = (f"d_ap{(p_-1) // 2}_{(p_-1) % 2}",) if p_ else ()
                rec.op("vector", lambda e, b=b, tt=tt: e.reduce_sum(
                    dcol[:, 0:1], x2_t[:, b, tt, :], AX.X),
                    sig=f"d_sum{c}_{tt}", wait=(f"d_ev{gl}",) + prev_ap)
                rec.op("scalar", lambda e, b=b, tt=tt: e.activation(
                    dsq[:, tt, :], x2_t[:, b, tt, :], AF.Square,
                    accum_out=dcol[:, 1:2]),
                    sig=f"d_ssq{c}_{tt}", wait=(f"d_ev{gl}",) + prev_ap)
                rec.op("vector", lambda e: e.tensor_scalar(
                    dcol[:, 2:3], dcol[:, 0:1], 1.0 / DM, None, ALU.mult),
                    sig=f"d_m{c}_{tt}",
                    wait=(f"d_sum{c}_{tt}", f"d_ssq{c}_{tt}"))
                rec.op("vector", lambda e: e.tensor_scalar(
                    dcol[:, 3:4], dcol[:, 1:2], 1.0 / DM, None, ALU.mult),
                    sig=f"d_ms{c}_{tt}", wait=(f"d_ssq{c}_{tt}",))
                rec.op("vector", lambda e: e.tensor_tensor(
                    dcol[:, 4:5], dcol[:, 2:3], dcol[:, 2:3], ALU.mult),
                    sig=f"d_sqm{c}_{tt}", wait=(f"d_m{c}_{tt}",))
                rec.op("vector", lambda e: e.tensor_tensor(
                    dcol[:, 4:5], dcol[:, 3:4], dcol[:, 4:5], ALU.subtract),
                    sig=f"d_var{c}_{tt}",
                    wait=(f"d_ms{c}_{tt}", f"d_sqm{c}_{tt}"))
                rec.op("scalar", lambda e: e.activation(
                    dcol[:, 5:6], dcol[:, 4:5], AF.Sqrt,
                    bias=eps_sb[:, 0:1], scale=1.0), sig=f"d_sd{c}_{tt}",
                    wait=(f"d_var{c}_{tt}", "epsz"))
                rec.op("vector", lambda e: e.reciprocal(
                    dcol[:, 6:7], dcol[:, 5:6]),
                    sig=f"d_rs{c}_{tt}", wait=(f"d_sd{c}_{tt}",))
                rec.op("vector", lambda e: e.tensor_tensor(
                    dcol[:, 7:8], dcol[:, 2:3], dcol[:, 6:7], ALU.mult),
                    sig=f"d_nm0{c}_{tt}", wait=(f"d_rs{c}_{tt}",))
                rec.op("vector", lambda e: e.tensor_scalar(
                    dcol[:, 7:8], dcol[:, 7:8], -1.0, None, ALU.mult),
                    sig=f"d_nm{c}_{tt}", wait=(f"d_nm0{c}_{tt}",))
                wap = [f"d_nm{c}_{tt}"]
                if c >= 2:
                    wap.append(f"d_st{c-2}")
                rec.op("vector", lambda e, b=b, tt=tt: e.tensor_scalar(
                    o_sb[:, b, tt, :], x2_t[:, b, tt, :], dcol[:, 6:7],
                    dcol[:, 7:8], ALU.mult, ALU.add),
                    sig=f"d_ap{c}_{tt}", wait=tuple(wap))
            rec.op("sync", lambda e, c=c, b=b: e.dma_start(
                out=out_h[c * CC:(c + 1) * CC, :].rearrange(
                    "(a p) d -> p a d", p=128),
                in_=o_sb[:, b, :, :]),
                sig=f"d_st{c}", wait=(f"d_ap{c}_0", f"d_ap{c}_1"))

        for c in range(NCC):
            d_chunk(c)

        if "scal" in dbg:
            rec.op("sync", lambda e: e.dma_start(
                out=dbg_scal.ap(), in_=sc_sb[:]), sig="dbg_sc",
                wait=("ar3_sc",))

        rec.op("sync", lambda e: None,
               wait=tuple(f"d_st{c}" for c in range(NCC)))

        esD.close()
        with nc.Block() as block:
            rec.emit(block, sems, prefix=prefix)

    return nc


_NC_CACHE = {}


def _get_nc(dbg=()):
    key = tuple(sorted(dbg))
    if key not in _NC_CACHE:
        _NC_CACHE[key] = build(dbg)
    return _NC_CACHE[key]


def _wslice(wu, wg, wd, r):
    tiles = []
    for g in range(9 * r, 9 * r + 9):
        if g < 24:
            tiles.append(wu[g * 128:(g + 1) * 128, :])
        elif g < 48:
            t = g - 24
            tiles.append(wg[t * 128:(t + 1) * 128, :])
        else:
            s, t = (g - 48) // 8, (g - 48) % 8
            tiles.append(wd[t * 128:(t + 1) * 128, s * DM:(s + 1) * DM])
    return np.ascontiguousarray(np.concatenate(tiles, axis=0))


def kernel(L0_attn_u, L0_attn_log_lambda, L0_attn_B_w, L0_attn_C_w,
           L0_attn_D, L0_ffn_w_gate, L0_ffn_w_up, L0_ffn_w_down):
    u = np.ascontiguousarray(np.asarray(L0_attn_u, dtype=np.float32))
    ll = np.ascontiguousarray(np.asarray(L0_attn_log_lambda,
                                         dtype=np.float32))
    bw = np.ascontiguousarray(np.asarray(L0_attn_B_w, dtype=np.float32))
    cw = np.ascontiguousarray(np.asarray(L0_attn_C_w, dtype=np.float32))
    dv = np.ascontiguousarray(np.asarray(L0_attn_D, dtype=np.float32))
    wg = np.ascontiguousarray(np.asarray(L0_ffn_w_gate, dtype=np.float32))
    wu = np.ascontiguousarray(np.asarray(L0_ffn_w_up, dtype=np.float32))
    wd = np.ascontiguousarray(np.asarray(L0_ffn_w_down, dtype=np.float32))

    nc = _get_nc()
    in_maps = []
    for r in range(8):
        in_maps.append(dict(u=np.ascontiguousarray(u[r]), ll=ll, bw=bw,
                            cw=cw, dv=dv, wslice=_wslice(wu, wg, wd, r)))
    res = run_bass_kernel_spmd(nc, in_maps, core_ids=list(range(8)))
    out = np.stack([res.results[r]["out"] for r in range(8)], axis=0)
    return out.astype(np.float32)


# version 16
# speedup vs baseline: 1.0296x; 1.0296x over previous
import numpy as np

import concourse.bass as bass
import concourse.mybir as mybir
from concourse.bass_utils import run_bass_kernel_spmd

F32 = mybir.dt.float32
BF16 = mybir.dt.bfloat16
F16 = mybir.dt.float16
U16 = mybir.dt.uint16
AF = mybir.ActivationFunctionType
ALU = mybir.AluOpType
AX = mybir.AxisListType

TOK = 8192
DM = 1024
NST = 256
DI = 3072
ND = DM // 128
NI = DI // 128
NH = NST // 128
CA = 512
NCA = TOK // CA
CC = 256
NCC = TOK // CC
QMAX = 32767.0
SHIFT = 32768.0
ROFF = 32768.5

ENG_SEM = {
    "sync": ("dS", 16, 8),
    "synct": ("dT", 16, 4),
    "scalt": ("dAT", 16, 4),
    "gdma": ("dG", 16, 8),
    "tensor": ("cT", 1, 1),
    "vector": ("cV", 1, 1),
    "scalar": ("cS", 1, 1),
    "gpsimd": ("cG", 1, 1),
}


def sem_names():
    out = []
    for base, step, pool in ENG_SEM.values():
        if pool == 1:
            out.append(base)
        else:
            out += [f"{base}{i}" for i in range(pool)]
    return out


class Rec:

    def __init__(self):
        self.ops = []
        self.counts = {s: 0 for s in sem_names()}
        self.rr = {k: 0 for k in ENG_SEM}
        self.events = {}

    def op(self, eng, fn, sig=None, wait=()):
        base, step, pool = ENG_SEM[eng]
        if sig is not None:
            if pool == 1:
                sem = base
            else:
                sem = f"{base}{self.rr[eng] % pool}"
                self.rr[eng] += 1
            self.counts[sem] += step
            assert sig not in self.events, sig
            self.events[sig] = (sem, self.counts[sem])
        if isinstance(wait, str):
            wait = (wait,)
        self.ops.append((eng, fn, sig, tuple(wait)))

    def emit(self, block, sems, prefix=None):
        ops = self.ops if prefix is None else self.ops[:prefix]

        def body_for(keys):
            def body(eng):
                last = {}
                for key, fn, sig, waits in ops:
                    if key not in keys:
                        continue
                    for w in waits:
                        s, v = self.events[w]
                        if last.get(s, 0) < v:
                            eng.wait_ge(sems[s], v)
                            last[s] = v
                    instr = fn(eng)
                    if sig is not None:
                        sem, _ = self.events[sig]
                        _, step, _ = ENG_SEM[key]
                        instr.then_inc(sems[sem], step)
            return body

        block.sync(body_for(("sync", "synct")))
        block.tensor(body_for(("tensor",)))
        block.vector(body_for(("vector",)))
        block.scalar(body_for(("scalar", "scalt")))
        block.gpsimd(body_for(("gpsimd", "gdma")))


def s0ap(handle, offset, pstride, pcount, free):
    return bass.AP(tensor=handle.ap().tensor, offset=offset,
                   ap=[[pstride, pcount]] + list(free))


def build(dbg=(), max_phase=4, prefix=None):
    nc = bass.Bass("TRN2", target_bir_lowering=False)
    dbg = set(dbg)

    u_h = nc.dram_tensor("u", [TOK, DM], F32, kind="ExternalInput").ap()
    ll_h = nc.dram_tensor("ll", [NST], F32, kind="ExternalInput").ap()
    bw_h = nc.dram_tensor("bw", [NST, DM], F32, kind="ExternalInput").ap()
    cw_h = nc.dram_tensor("cw", [DM, NST], F32, kind="ExternalInput").ap()
    dv_h = nc.dram_tensor("dv", [DM], F32, kind="ExternalInput").ap()
    wsl_h = nc.dram_tensor("wslice", [9 * 128, DM], F32,
                           kind="ExternalInput").ap()
    out_h = nc.dram_tensor("out", [TOK, DM], F32, kind="ExternalOutput").ap()

    def scratch(name, shape, dtype):
        kind = "ExternalOutput" if name in dbg else "Internal"
        return nc.dram_tensor(name, shape, dtype, kind=kind)

    x1s = scratch("x1s", [DM, TOK], F16)
    uis = scratch("uis", [DI, TOK], F16)
    gis = scratch("gis", [DI, TOK], F16)
    g2s = scratch("g2s", [DI, TOK], F16)
    ops_ = scratch("ops", [DI, TOK], F16)
    NT_W = 72
    wag_in = nc.dram_tensor("wag_in", [NT_W // 8 * 128, DM], F16)
    wag = scratch("wag", [NT_W * 128, DM], F16)
    cc_bufs = [(nc.dram_tensor(f"cci{i}", [1, 8], F32),
                nc.dram_tensor(f"cco{i}", [1, 8], F32)) for i in range(4)]
    if "wagdbg" in dbg:
        wag_dbg = nc.dram_tensor("wag_dbg", [NT_W * 128, DM], F16,
                                 kind="ExternalOutput")
    if "scal" in dbg:
        dbg_scal = nc.dram_tensor("scal_dbg", [1, 16], F32,
                                  kind="ExternalOutput")

    rec = Rec()
    from contextlib import ExitStack
    with ExitStack() as es:
        ec = es.enter_context
        sems = {}
        for s in sem_names():
            sems[s] = ec(nc.semaphore(s))

        P = [ec(nc.psum_tensor(f"P{i}", [128, 512], F32)) for i in range(8)]

        ident = ec(nc.sbuf_tensor("ident", [128, 128], F32))
        ones_mat = ec(nc.sbuf_tensor("ones_mat", [128, 128], F16))
        redm = ec(nc.sbuf_tensor("redm", [128, 128], F32))
        onesf = ec(nc.sbuf_tensor("onesf", [1, 128], F32))
        gm2 = ec(nc.sbuf_tensor("gm2", [128, 128], F32))
        ll_sb = ec(nc.sbuf_tensor("ll_sb", [128, NH], F32))
        lam_sb = ec(nc.sbuf_tensor("lam_sb", [128, NH], F32))
        tmp_lam = ec(nc.sbuf_tensor("tmp_lam", [128, NH], F32))
        dv_sb = ec(nc.sbuf_tensor("dv_sb", [128, ND], F32))
        onepD = ec(nc.sbuf_tensor("onepD", [128, ND], F32))

        iota_p = ec(nc.sbuf_tensor("iota_p", [128, 128], F32))
        iota_f = ec(nc.sbuf_tensor("iota_f", [128, 128], F32))
        rmax_x1 = ec(nc.sbuf_tensor("rmax_x1", [128, 2], F32))
        rmax_ui = ec(nc.sbuf_tensor("rmax_ui", [128, 2], F32))
        rmax_gi = ec(nc.sbuf_tensor("rmax_gi", [128, 2], F32))
        rmax_g2 = ec(nc.sbuf_tensor("rmax_g2", [128, 2], F32))
        rmax_o = ec(nc.sbuf_tensor("rmax_o", [128, 2], F32))
        acol = ec(nc.sbuf_tensor("acol", [128, 4], F32))
        redc = ec(nc.sbuf_tensor("redc", [128, 1], F32))
        gmax_row = ec(nc.sbuf_tensor("gmax_row", [1, 128], F32))
        cc_sb = ec(nc.sbuf_tensor("cc_sb", [1, 8], F32))
        ccr_sb = ec(nc.sbuf_tensor("ccr_sb", [1, 8], F32))
        pp1 = ec(nc.sbuf_tensor("pp1", [128, 8], F32))
        pp2 = ec(nc.sbuf_tensor("pp2", [128, 8], F32))
        pp3 = ec(nc.sbuf_tensor("pp3", [128, 8], F32))
        pp4 = ec(nc.sbuf_tensor("pp4", [128, 8], F32))
        sc_sb = ec(nc.sbuf_tensor("sc_sb", [1, 16], F32))
        eps_sb = ec(nc.sbuf_tensor("eps_sb", [128, 1], F32))

        esRun = ExitStack()
        runmx = esRun.enter_context(nc.sbuf_tensor("runmx", [128, 5632], F16))
        runmn = esRun.enter_context(nc.sbuf_tensor("runmn", [128, 5632], F16))
        esA = ExitStack()
        ea = esA.enter_context
        bw_nat = ea(nc.sbuf_tensor("bw_nat", [128, NH, DM], F32))
        cw_nat = ea(nc.sbuf_tensor("cw_nat", [128, ND, NST], F32))
        bwT = ea(nc.sbuf_tensor("bwT", [128, ND, NST], F16))
        cwT = ea(nc.sbuf_tensor("cwT", [128, NH, DM], BF16))

        rec.op("sync", lambda e: e.dma_start(
            out=ll_sb[:], in_=ll_h.rearrange("(j p) -> p j", p=128),
            allow_slow_non_contiguous=True), sig="ld_ll")
        rec.op("sync", lambda e: e.dma_start(
            out=dv_sb[:], in_=dv_h.rearrange("(k p) -> p k", p=128),
            allow_slow_non_contiguous=True), sig="ld_dv")
        rec.op("sync", lambda e: e.dma_start(
            out=bw_nat[:, :, :],
            in_=bw_h.rearrange("(j p) d -> p j d", p=128)), sig="ld_bw")
        rec.op("sync", lambda e: e.dma_start(
            out=cw_nat[:, :, :],
            in_=cw_h.rearrange("(k p) n -> p k n", p=128)), sig="ld_cw")

        rec.op("gpsimd", lambda e: e.iota(
            iota_p[:], [[0, 128]], base=0, channel_multiplier=1,
            allow_small_or_imprecise_dtypes=True), sig="iop")
        rec.op("gpsimd", lambda e: e.iota(
            iota_f[:], [[1, 128]], base=0, channel_multiplier=0,
            allow_small_or_imprecise_dtypes=True), sig="iof")
        rec.op("vector", lambda e: e.tensor_tensor(
            ident[:], iota_p[:], iota_f[:], ALU.is_equal),
            sig="ident", wait=("iop", "iof"))
        rec.op("vector", lambda e: e.memset(ones_mat[:], 1.0), sig="ones_c")
        rec.op("vector", lambda e: e.memset(redm[:], 0.0), sig="redmz")
        rec.op("vector", lambda e: e.memset(onesf[:], 1.0), sig="onesf")
        rec.op("vector", lambda e: e.memset(cc_sb[:], 0.0), sig="ccz")
        rec.op("vector", lambda e: e.memset(eps_sb[:], 1e-5), sig="epsz")
        for nm, t in (("z1", rmax_x1), ("z2", rmax_ui), ("z3", rmax_gi),
                      ("z4", rmax_g2), ("z5", rmax_o)):
            rec.op("vector", lambda e, t=t: e.memset(t[:], 0.0), sig=nm)
        rec.op("vector", lambda e: e.memset(runmx[:], -65504.0), sig="zmx")
        rec.op("vector", lambda e: e.memset(runmn[:], 65504.0), sig="zmn")

        rec.op("scalar", lambda e: e.activation(
            tmp_lam[:], ll_sb[:], AF.Exp, bias=0.0, scale=-1.0),
            sig="lamexp", wait="ld_ll")
        rec.op("vector", lambda e: e.tensor_scalar(
            tmp_lam[:], tmp_lam[:], 1.0, None, ALU.add),
            sig="lamadd", wait="lamexp")
        rec.op("vector", lambda e: e.reciprocal(lam_sb[:], tmp_lam[:]),
               sig="lam", wait="lamadd")
        rec.op("vector", lambda e: e.tensor_scalar(
            onepD[:], dv_sb[:], 1.0, None, ALU.add), sig="onepD",
            wait="ld_dv")

        tc = 0
        for j in range(NH):
            for k in range(ND):
                nmT, nmS = f"bwT_t{tc}", f"bwT_s{tc}"
                w = ["ld_bw", "ident"] + ([f"bwT_s{tc-1}"] if tc else [])
                rec.op("tensor", lambda e, j=j, k=k: e.transpose(
                    P[6][:, 0:128], bw_nat[:, j, k * 128:(k + 1) * 128],
                    ident[:]), sig=nmT, wait=tuple(w))
                rec.op("scalar", lambda e, j=j, k=k: e.copy(
                    bwT[:, k, j * 128:(j + 1) * 128], P[6][:, 0:128]),
                    sig=nmS, wait=nmT)
                tc += 1
        tc = 0
        for k in range(ND):
            for j in range(NH):
                nmT, nmS = f"cwT_t{tc}", f"cwT_s{tc}"
                w = ["ld_cw", "ident"] + ([f"cwT_s{tc-1}"] if tc else [])
                rec.op("tensor", lambda e, j=j, k=k: e.transpose(
                    P[7][:, 0:128], cw_nat[:, k, j * 128:(j + 1) * 128],
                    ident[:]), sig=nmT, wait=tuple(w))
                rec.op("scalar", lambda e, j=j, k=k: e.copy(
                    cwT[:, j, k * 128:(k + 1) * 128], P[7][:, 0:128]),
                    sig=nmS, wait=nmT)
                tc += 1

        wq_nat = ea(nc.sbuf_tensor("wq_nat", [128, 2, DM], F32))
        wq_bm = ea(nc.sbuf_tensor("wq_bm", [128, 2, 32], F32))
        wq_s = ea(nc.sbuf_tensor("wq_s", [128, 2, 32], F32))
        wq_si = ea(nc.sbuf_tensor("wq_si", [128, 2, 32], F32))
        wq_t = ea(nc.sbuf_tensor("wq_t", [128, 2, DM], F32))
        wq_u = ea(nc.sbuf_tensor("wq_u", [128, 2, DM], U16))
        wq_o = ea(nc.sbuf_tensor("wq_o", [128, 2, DM], F16))

        def dequant_tile(s):
            b = s % 2
            grp = []
            w = [f"wq_sp{s-2}"] if s >= 2 else []
            grp.append(("sync", lambda e, s=s, b=b: e.dma_start(
                out=wq_nat[:, b, :], in_=wsl_h[s * 128:(s + 1) * 128, :]),
                f"wq_ld{s}", tuple(w)))
            w2 = [f"wq_ld{s}"] + ([f"wq_si{s-2}"] if s >= 2 else [])
            grp.append(("vector", lambda e, b=b: e.tensor_reduce(
                wq_bm[:, b, :], wq_nat.ap().rearrange(
                    "p a (bl el) -> p a bl el", el=32)[:, b], AX.X, ALU.max,
                apply_absolute_value=True), f"wq_bm{s}", tuple(w2)))
            w3 = [f"wq_bm{s}"] + ([f"wq_o{s-2}"] if s >= 2 else [])
            grp.append(("vector", lambda e, b=b: e.tensor_scalar(
                wq_s[:, b, :], wq_bm[:, b, :], 1.0 / 7.0, 1e-12, ALU.mult,
                ALU.max), f"wq_s{s}", tuple(w3)))
            grp.append(("vector", lambda e, b=b: e.reciprocal(
                wq_si[:, b, :], wq_s[:, b, :]), f"wq_si{s}", (f"wq_s{s}",)))

            def mul_bcast(e, b=b):
                sib = s0ap(wq_si, b * 32, 64, 128, [[1, 32], [0, 32]])
                return e.tensor_tensor(
                    wq_t.ap().rearrange("p a (bl el) -> p a bl el",
                                        el=32)[:, b],
                    wq_nat.ap().rearrange("p a (bl el) -> p a bl el",
                                          el=32)[:, b],
                    sib, ALU.mult)
            w4 = [f"wq_si{s}"] + ([f"wq_q{s-2}"] if s >= 2 else [])
            grp.append(("vector", mul_bcast, f"wq_t{s}", tuple(w4)))
            WOFF = 8.0
            grp.append(("vector", lambda e, b=b: e.tensor_scalar(
                wq_u[:, b, :], wq_t[:, b, :], 1.0, WOFF, ALU.mult, ALU.add),
                f"wq_u{s}", (f"wq_t{s}",)))
            grp.append(("vector", lambda e, b=b: e.scalar_tensor_tensor(
                wq_nat[:, b, :], wq_u[:, b, :], -WOFF, wq_t[:, b, :],
                ALU.add, ALU.subtract), f"wq_e{s}", (f"wq_u{s}",)))
            grp.append(("vector", lambda e, b=b: e.tensor_scalar(
                wq_t[:, b, :], wq_nat[:, b, :], -0.5, None, ALU.is_le),
                f"wq_c{s}", (f"wq_e{s}",)))
            grp.append(("vector", lambda e, b=b: e.scalar_tensor_tensor(
                wq_nat[:, b, :], wq_u[:, b, :], -WOFF, wq_t[:, b, :],
                ALU.add, ALU.add), f"wq_q{s}", (f"wq_c{s}",)))

            def deq_mul(e, b=b):
                sb = s0ap(wq_s, b * 32, 64, 128, [[1, 32], [0, 32]])
                return e.tensor_tensor(
                    wq_o.ap().rearrange("p a (bl el) -> p a bl el",
                                        el=32)[:, b],
                    wq_nat.ap().rearrange("p a (bl el) -> p a bl el",
                                          el=32)[:, b],
                    sb, ALU.mult)
            grp.append(("vector", deq_mul, f"wq_o{s}", (f"wq_q{s}",)))
            grp.append(("gdma", lambda e, s=s, b=b: e.dma_start(
                out=wag_in[s * 128:(s + 1) * 128, :], in_=wq_o[:, b, :]),
                f"wq_sp{s}", (f"wq_o{s}",)))
            return grp

        dequant_ops = [dequant_tile(s) for s in range(9)]

        u_nat = ea(nc.sbuf_tensor("u_nat", [128, 2, 4, DM], F32))
        uT16 = ea(nc.sbuf_tensor("uT16", [128, 2, ND, CA], F16))
        h32 = ea(nc.sbuf_tensor("h32", [128, 2, NH, CA], F32))
        h16 = ea(nc.sbuf_tensor("h16", [128, 2, NH, CA], BF16))
        h16lo = ea(nc.sbuf_tensor("h16lo", [128, 2, NH, CA], BF16))
        x1pre = ea(nc.sbuf_tensor("x1pre", [128, 2, ND, CA], F16))
        x1sq = ea(nc.sbuf_tensor("x1sq", [128, ND, CA], F16))
        ab_b = ea(nc.sbuf_tensor("ab_b", [128, 2, 2, CA], F16))
        x1out = ea(nc.sbuf_tensor("x1out", [128, 2, ND, CA], F16))
        tmpa = ea(nc.sbuf_tensor("tmpa", [128, 2, CA], F16))
        stf = ea(nc.sbuf_tensor("stf", [128, 4, CA], F32))

        def phase_a_chunk(c):
            b = c % 2
            w = [f"a_tr{c-2}_{ND-1}"] if c >= 2 else []
            rec.op("sync", lambda e, c=c, b=b: e.dma_start(
                out=u_nat[:, b, :, :],
                in_=u_h[c * CA:(c + 1) * CA, :].rearrange(
                    "(a p) d -> p a d", p=128)),
                sig=f"a_u{c}", wait=tuple(w))
            for k in range(ND):
                pb = P[k % 2]
                gk = c * ND + k
                tw = [f"a_u{c}"]
                if gk >= 2:
                    pk = gk - 2
                    tw.append(f"a_uev{pk // ND}_{pk % ND}")
                if c == 0 and k == 0:
                    tw.append(f"bwT_s{2*ND-1}")
                for a in range(4):
                    rec.op("tensor", lambda e, b=b, k=k, a=a, pb=pb:
                           e.transpose(
                               pb[:, a * 128:(a + 1) * 128],
                               u_nat[:, b, a, k * 128:(k + 1) * 128],
                               ident[:]),
                           sig=f"a_tr{c}_{k}" if a == 3 else None,
                           wait=tuple(tw) if a == 0 else ())
                ew = [f"a_tr{c}_{k}"]
                if c >= 2:
                    ew.append(f"a_res{c-2}_{k}")
                rec.op("scalar", lambda e, b=b, k=k, pb=pb: e.copy(
                    uT16[:, b, k, :], pb[:]),
                    sig=f"a_uev{c}_{k}", wait=tuple(ew))
            for nh in range(NH):
                pb = P[2 + nh]
                for k in range(ND):
                    w2 = [f"a_uev{c}_{ND-1}"]
                    if c >= 1 and k == 0:
                        w2.append(f"a_scan{c-1}_{nh}")
                    rec.op("tensor", lambda e, b=b, nh=nh, k=k, pb=pb:
                           e.matmul(
                               pb[:], bwT[:, k, nh * 128:(nh + 1) * 128],
                               uT16[:, b, k, :], start=(k == 0),
                               stop=(k == ND - 1)),
                           sig=f"a_bu{c}_{nh}" if k == ND - 1 else None,
                           wait=tuple(w2) if k == 0 else ())
            for nh in range(NH):
                lamb = s0ap(lam_sb, nh, NH, 128, [[0, CA]])
                init = 0.0 if c == 0 else h32[:, 1 - b, nh, CA - 1:CA]
                w2 = [f"a_bu{c}_{nh}"]
                if c == 0:
                    w2.append("lam")
                if c >= 1:
                    w2.append(f"a_h16{c-1}_{nh}")
                rec.op("vector", lambda e, b=b, nh=nh, lamb=lamb, init=init:
                       e.tensor_tensor_scan(
                           h32[:, b, nh, :], lamb, P[2 + nh][:], init,
                           ALU.mult, ALU.add),
                       sig=f"a_scan{c}_{nh}", wait=tuple(w2))
                w3 = [f"a_scan{c}_{nh}"]
                if c >= 2:
                    w3.append(f"a_y{c-2}_{ND-1}")
                rec.op("vector", lambda e, b=b, nh=nh: e.tensor_copy(
                    h16[:, b, nh, :], h32[:, b, nh, :]),
                    sig=f"a_h16h{c}_{nh}", wait=tuple(w3))
                rec.op("vector", lambda e, b=b, nh=nh: e.tensor_tensor(
                    h16lo[:, b, nh, :], h32[:, b, nh, :], h16[:, b, nh, :],
                    ALU.subtract), sig=f"a_h16{c}_{nh}",
                    wait=(f"a_h16h{c}_{nh}",))
            for k in range(ND):
                pb = P[4 + (k % 2)]
                gk = c * ND + k
                w2 = [f"a_h16{c}_{NH-1}"]
                if gk >= 2:
                    pk = gk - 2
                    w2.append(f"a_res{pk // ND}_{pk % ND}")
                for j, (nh, hsrc) in enumerate(
                        [(n_, h_) for n_ in range(NH)
                         for h_ in (h16, h16lo)]):
                    rec.op("tensor", lambda e, b=b, k=k, nh=nh, hsrc=hsrc,
                           pb=pb, j=j: e.matmul(
                               pb[:], cwT[:, nh, k * 128:(k + 1) * 128],
                               hsrc[:, b, nh, :], start=(j == 0),
                               stop=(j == 2 * NH - 1)),
                           sig=f"a_y{c}_{k}" if j == 2 * NH - 1 else None,
                           wait=tuple(w2) if j == 0 else ())
                w3 = [f"a_y{c}_{k}"]
                if c == 0:
                    w3.append("onepD")
                if c >= 2:
                    w3.append(f"a_ap1_{c-2}_{k}")
                rec.op("vector", lambda e, b=b, k=k, pb=pb:
                       e.scalar_tensor_tensor(
                           x1pre[:, b, k, :], uT16[:, b, k, :],
                           onepD[:, k:k + 1], pb[:], ALU.mult, ALU.add),
                       sig=f"a_res{c}_{k}", wait=tuple(w3))
                w4 = [f"a_res{c}_{k}"]
                if c >= 1 and k == 0:
                    w4.append(f"a_stmm{c-1}")
                rec.op("scalar", lambda e, k=k: e.activation(
                    x1sq[:, k, :], x1pre[:, b, k, :], AF.Square),
                    sig=f"a_sq{c}_{k}", wait=tuple(w4))
            for k in range(ND):
                w2 = []
                if k == 0:
                    w2 = [f"a_res{c}_{ND-1}", f"a_sq{c}_{ND-1}", "ones_c"]
                    if c == 0:
                        w2 += [f"bwT_s{2*ND-1}", f"cwT_s{2*ND-1}"]
                    else:
                        w2 += [f"a_m{c-1}", f"a_ms{c-1}"]
                rec.op("tensor", lambda e, b=b, k=k: e.matmul(
                    P[6][:], ones_mat[:], x1pre[:, b, k, :],
                    start=(k == 0), stop=(k == ND - 1)),
                    wait=tuple(w2))
                rec.op("tensor", lambda e, k=k: e.matmul(
                    P[7][:], ones_mat[:], x1sq[:, k, :],
                    start=(k == 0), stop=(k == ND - 1)),
                    sig=f"a_stmm{c}" if k == ND - 1 else None)
            w2 = [f"a_stmm{c}"]
            if c >= 1:
                w2 += [f"a_ab0_{c-1}", f"a_ab1_{c-1}"]
            rec.op("vector", lambda e: e.tensor_scalar(
                stf[:, 0, :], P[6][:], 1.0 / DM, None, ALU.mult),
                sig=f"a_m{c}", wait=tuple(w2))
            rec.op("vector", lambda e: e.tensor_scalar(
                stf[:, 1, :], P[7][:], 1.0 / DM, None, ALU.mult),
                sig=f"a_ms{c}", wait=tuple(w2))
            rec.op("vector", lambda e: e.tensor_tensor(
                stf[:, 2, :], stf[:, 0, :], stf[:, 0, :], ALU.mult),
                sig=f"a_m2{c}", wait=(f"a_m{c}",))
            rec.op("vector", lambda e: e.tensor_tensor(
                stf[:, 3, :], stf[:, 1, :], stf[:, 2, :], ALU.subtract),
                sig=f"a_var{c}", wait=(f"a_ms{c}", f"a_m2{c}"))
            rec.op("scalar", lambda e: e.activation(
                stf[:, 1, :], stf[:, 3, :], AF.Sqrt,
                bias=eps_sb[:, 0:1], scale=1.0), sig=f"a_sd{c}",
                wait=(f"a_var{c}", "epsz"))
            rec.op("vector", lambda e: e.reciprocal(
                stf[:, 2, :], stf[:, 1, :]), sig=f"a_rstd{c}",
                wait=(f"a_sd{c}",))
            rec.op("vector", lambda e: e.tensor_tensor(
                stf[:, 3, :], stf[:, 0, :], stf[:, 2, :], ALU.mult),
                sig=f"a_mr{c}", wait=(f"a_rstd{c}",))
            w2 = [f"a_rstd{c}"]
            if c >= 2:
                w2.append(f"a_ap1_{c-2}_{ND-1}")
            rec.op("vector", lambda e, b=b: e.tensor_copy(
                ab_b[:, b, 0, :], stf[:, 2, :]), sig=f"a_ab0_{c}",
                wait=tuple(w2))
            w2 = [f"a_mr{c}"]
            if c >= 2:
                w2.append(f"a_ap2_{c-2}_{ND-1}")
            rec.op("vector", lambda e, b=b: e.tensor_copy(
                ab_b[:, b, 1, :], stf[:, 3, :]), sig=f"a_ab1_{c}",
                wait=tuple(w2))
            for k in range(ND):
                w2 = [f"a_ab0_{c}", f"a_ab1_{c}"]
                if k > 0:
                    w2.append(f"a_ap2_{c}_{k-1}")
                if c >= 2 and k == 0:
                    w2 += [f"a_sp{c-2}", f"a_mx{c-2}"]
                rec.op("vector", lambda e, b=b, k=k: e.tensor_tensor(
                    tmpa[:, b, :], x1pre[:, b, k, :], ab_b[:, b, 0, :],
                    ALU.mult), sig=f"a_ap1_{c}_{k}", wait=tuple(w2))
                rec.op("vector", lambda e, b=b, k=k: e.tensor_tensor(
                    x1out[:, b, k, :], tmpa[:, b, :], ab_b[:, b, 1, :],
                    ALU.subtract), sig=f"a_ap2_{c}_{k}",
                    wait=(f"a_ap1_{c}_{k}",))
            full = x1out.ap().rearrange("p a k t -> p a (k t)")
            W = ND * CA
            w2 = [f"a_ap2_{c}_{ND-1}", "zmx"]
            if c >= 1:
                w2.append(f"a_mx{c-1}")
            rec.op("vector", lambda e, b=b, full=full, W=W: e.tensor_tensor(
                runmx[:, 0:W], full[:, b], runmx[:, 0:W], ALU.max),
                sig=f"a_mxp{c}", wait=tuple(w2))
            rec.op("vector", lambda e, b=b, full=full, W=W: e.tensor_tensor(
                runmn[:, 0:W], full[:, b], runmn[:, 0:W], ALU.min),
                sig=f"a_mx{c}", wait=(f"a_mxp{c}", "zmn"))
            rec.op("gdma", lambda e, c=c, b=b: e.dma_start(
                out=x1s.ap().rearrange("(k p) T -> p k T", p=128)[
                    :, :, c * CA:(c + 1) * CA],
                in_=x1out[:, b, :, :]),
                sig=f"a_sp{c}", wait=(f"a_ap2_{c}_{ND-1}",))

        for c in range(NCA):
            phase_a_chunk(c)
            if c < 9:
                for eng, fn, sig, w in dequant_ops[c]:
                    rec.op(eng, fn, sig=sig, wait=w)

        rec.op("gpsimd", lambda e: e.collective_compute(
            "AllGather", ALU.bypass, replica_groups=[list(range(8))],
            ins=[wag_in.ap().opt()], outs=[wag.ap().opt()]),
            sig="w_ag", wait=tuple(f"wq_sp{s}" for s in range(9)))
        if "wagdbg" in dbg:
            rec.op("gdma", lambda e: e.dma_start(
                out=wag_dbg.ap(), in_=wag.ap()), sig="w_agdbg",
                wait=("w_ag",))


        def fold_runs(pre, W, rmax, wait_ev):
            rec.op("vector", lambda e, W=W, rmax=rmax: e.tensor_reduce(
                rmax[:, 0:1], runmx[:, 0:W], AX.X, ALU.max),
                sig=f"{pre}_fmx", wait=wait_ev)
            rec.op("vector", lambda e, W=W, rmax=rmax: e.tensor_reduce(
                rmax[:, 1:2], runmn[:, 0:W], AX.X, ALU.min, negate=True),
                sig=f"{pre}_fmn", wait=(f"{pre}_fmx",))
            rec.op("vector", lambda e, W=W: e.memset(runmx[:, 0:W], -1e30),
                sig=f"{pre}_zx", wait=(f"{pre}_fmx",))
            rec.op("vector", lambda e, W=W: e.memset(runmn[:, 0:W], 1e30),
                sig=f"{pre}_zn", wait=(f"{pre}_fmn",))

        def sc(i):
            return sc_sb[0:1, i:i + 1]

        def local_gmax(pre, rmax, lane, wait_ev, tr_wait):
            rec.op("vector", lambda e, rmax=rmax: e.tensor_reduce(
                redc[:], rmax[:], AX.X, ALU.max), sig=f"{pre}_red",
                wait=wait_ev)
            rec.op("vector", lambda e: e.tensor_copy(
                redm[:, 0:1], redc[:]), sig=f"{pre}_rm",
                wait=(f"{pre}_red", "redmz"))
            rec.op("tensor", lambda e: e.transpose(
                P[6][:, 0:128], redm[:], ident[:]), sig=f"{pre}_tr",
                wait=(f"{pre}_rm",) + tuple(tr_wait))
            rec.op("vector", lambda e: e.tensor_copy(
                gm2[:], P[6][:, 0:128]), sig=f"{pre}_ev",
                wait=(f"{pre}_tr",))
            rec.op("vector", lambda e, lane=lane: e.tensor_reduce(
                cc_sb[0:1, lane:lane + 1], gm2[0:1, :], AX.X, ALU.max),
                sig=f"{pre}_m", wait=(f"{pre}_ev", "ccz"))

        def do_cc(pre, idx, wait_ev):
            cin, cout = cc_bufs[idx]
            rec.op("gdma", lambda e, cin=cin: e.dma_start(
                out=cin.ap(), in_=cc_sb[:]), sig=f"{pre}_di", wait=wait_ev)
            rec.op("gpsimd", lambda e, cin=cin, cout=cout:
                   e.collective_compute(
                       "AllReduce", ALU.max, replica_groups=[list(range(8))],
                       ins=[cin.ap().opt()], outs=[cout.ap().opt()]),
                   sig=f"{pre}_cc", wait=(f"{pre}_di",))
            rec.op("gdma", lambda e, cout=cout: e.dma_start(
                out=ccr_sb[:], in_=cout.ap()), sig=f"{pre}_do",
                wait=(f"{pre}_cc",))

        def finish(pre, pp):
            rec.op("tensor", lambda e: e.matmul(
                P[6][:, 0:8], onesf[:], sc_sb[0:1, 0:8], start=True,
                stop=True), sig=f"{pre}_fmm",
                wait=(f"{pre}_sc", f"{pre}_ev", "onesf"))
            rec.op("vector", lambda e, pp=pp: e.tensor_copy(
                pp[:, 0:8], P[6][:, 0:8]), sig=f"{pre}_fin",
                wait=(f"{pre}_fmm",))

        fold_runs("ar0f", ND * CA, rmax_x1, (f"a_mx{NCA-1}",))
        local_gmax("ar0", rmax_x1, 0, ("ar0f_fmn",),
                   (f"a_m{NCA-1}", f"a_ms{NCA-1}"))
        do_cc("ar0", 0, (f"ar0_m",))
        rec.op("vector", lambda e: e.tensor_scalar(
            sc(3), ccr_sb[0:1, 0:1], 1.0 / QMAX, 1e-12, ALU.mult, ALU.max),
            sig="ar0_s1", wait=("ar0_do",))
        rec.op("vector", lambda e: e.reciprocal(sc(0), sc(3)),
               sig="ar0_sc", wait=("ar0_s1",))
        finish("ar0", pp1)
        esA.close()
        if max_phase < 1:
            rec.op("sync", lambda e: None, wait=("ar0_fin", f"a_sp{NCA-1}"))
            with nc.Block() as block:
                rec.emit(block, sems, prefix=prefix)
            return nc
        BAR_A = ("ar0_tr", "ar0_sc", f"a_sd{NCA-1}", "ar0_do", "ar0_fin",
                 f"a_u{NCA-1}", f"wq_ld{8}")

        esBc = ExitStack()
        ebc = esBc.enter_context
        x1r = ebc(nc.sbuf_tensor("x1r", [128, ND, CA], F16))
        x1u = ebc(nc.sbuf_tensor("x1u", [128, ND, CA], U16))
        x1i = ebc(nc.sbuf_tensor("x1i", [128, 2, ND, CA], F16))
        ev_ring = ebc(nc.sbuf_tensor("ev_ring", [128, 8, CA], F16))
        wuT = ebc(nc.sbuf_tensor("wuT", [128, ND, DI], F16))
        esB1 = ExitStack()
        wgT = esB1.enter_context(nc.sbuf_tensor("wgT", [128, ND, DI], F16))

        for k in range(ND):
            rec.op("synct", lambda e, k=k: e.dma_start_transpose(
                wgT[:, k, :], wag.ap()[24 * 128:48 * 128,
                                       k * 128:(k + 1) * 128]),
                sig=f"b_wg{k}", wait=("w_ag",) + BAR_A)
        for k in range(ND):
            rec.op("synct", lambda e, k=k: e.dma_start_transpose(
                wuT[:, k, :], wag.ap()[0:24 * 128, k * 128:(k + 1) * 128]),
                sig=f"b_wu{k}", wait=("w_ag",) + BAR_A)

        def b_prep_chunk(ph, c, wname):
            b = c % 2
            w = [f"a_sp{c}", "ar0_fin"]
            if ph == "b1" and c == 0:
                w += list(BAR_A)
            if ph == "b2" and c == 0:
                w.append(f"b1_mm{NCA * NI - 1}")
            if c >= 1:
                w.append(f"{ph}_x1u{c-1}")
            rec.op("sync", lambda e, c=c: e.dma_start(
                out=x1r[:, :, :],
                in_=x1s.ap().rearrange("(k p) T -> p k T", p=128)[
                    :, :, c * CA:(c + 1) * CA]),
                sig=f"{ph}_x1{c}", wait=tuple(w))
            fl = x1r.ap().rearrange("p k t -> p (k t)")
            fu = x1u.ap().rearrange("p k t -> p (k t)")
            fi = x1i.ap().rearrange("p a k t -> p a (k t)")
            w2 = [f"{ph}_x1{c}"]
            if c >= 1:
                w2.append(f"{ph}_x1i{c-1}")
            rec.op("vector", lambda e, fl=fl, fu=fu: e.tensor_scalar(
                fu, fl, pp1[:, 0:1], ROFF, ALU.mult, ALU.add),
                sig=f"{ph}_x1u{c}", wait=tuple(w2))
            w2 = [f"{ph}_x1u{c}"]
            if c >= 2:
                w2.append(f"{ph}_mm{(c-1)*NI-1}")
            rec.op("vector", lambda e, b=b, fu=fu, fi=fi: e.tensor_scalar(
                fi[:, b], fu, -SHIFT, None, ALU.add),
                sig=f"{ph}_x1i{c}", wait=tuple(w2))

        def b_phase_chunk(ph, c, wT, dst, wname):
            b = c % 2
            for it in range(NI):
                g = c * NI + it
                pb = P[g % 4]
                wt = [f"{ph}_x1i{c}",
                      f"b_wg{ND-1}" if ph == "b1" else f"b_wu{ND-1}"]
                if g >= 4:
                    wt.append(f"{ph}_ev{g-4}")
                for k in range(ND):
                    rec.op("tensor",
                           lambda e, b=b, wT=wT, it=it, k=k, pb=pb:
                           e.matmul(
                               pb[:], wT[:, k, it * 128:(it + 1) * 128],
                               x1i[:, b, k, :], start=(k == 0),
                               stop=(k == ND - 1)),
                           sig=f"{ph}_mm{g}" if k == ND - 1 else None,
                           wait=tuple(wt) if k == 0 else ())
                slot = g % 8
                wt2 = [f"{ph}_mm{g}"]
                if g >= 8:
                    j = (g - 8) // 8
                    wt2 += [f"{ph}_sp{j}", f"{ph}_mx{j}"]
                rec.op("scalar", lambda e, slot=slot, pb=pb:
                       e.activation(ev_ring[:, slot, :], pb[:], AF.Copy,
                                    bias=0.0, scale=0.125),
                       sig=f"{ph}_ev{g}", wait=tuple(wt2))
            for sp in range(3):
                g0 = c * NI + sp * 8
                j = g0 // 8
                it0 = sp * 8
                s0 = g0 % 8
                rec.op("gdma", lambda e, c=c, dst=dst, it0=it0, s0=s0:
                       e.dma_start(
                           out=dst.ap().rearrange(
                               "(i p) T -> p i T", p=128)[
                               :, it0:it0 + 8, c * CA:(c + 1) * CA],
                           in_=ev_ring[:, s0:s0 + 8, :]),
                       sig=f"{ph}_sp{j}", wait=(f"{ph}_ev{g0+7}",))
                w2 = [f"{ph}_ev{g0+7}", "ar0f_zx", "ar0f_zn"]
                if j >= 1:
                    w2.append(f"{ph}_mx{j-1}")
                rall = ev_ring.ap().rearrange("p s t -> p (s t)")[
                    :, s0 * CA:(s0 + 8) * CA]
                rec.op("vector", lambda e, rall=rall: e.tensor_tensor(
                    runmx[:, 0:8 * CA], rall, runmx[:, 0:8 * CA], ALU.max),
                    sig=f"{ph}_mxa{j}", wait=tuple(w2))
                rec.op("vector", lambda e, rall=rall: e.tensor_tensor(
                    runmn[:, 0:8 * CA], rall, runmn[:, 0:8 * CA], ALU.min),
                    sig=f"{ph}_mx{j}", wait=(f"{ph}_mxa{j}",))

        b_prep_chunk("b1", 0, "g")
        for c in range(NCA):
            if c + 1 < NCA:
                b_prep_chunk("b1", c + 1, "g")
            b_phase_chunk("b1", c, wgT, gis, "g")
        rec.op("vector", lambda e: e.tensor_reduce(
            rmax_gi[:, 0:1], runmx[:, 0:8 * CA], AX.X, ALU.max),
            sig="b1_fmx", wait=(f"b1_mx{NCA*3-1}",))
        rec.op("vector", lambda e: e.tensor_reduce(
            rmax_gi[:, 1:2], runmn[:, 0:8 * CA], AX.X, ALU.min, negate=True),
            sig="b1_fmn", wait=("b1_fmx",))
        rec.op("vector", lambda e: e.memset(runmx[:, 0:8 * CA], -1e30),
               sig="b1_zx", wait=("b1_fmx",))
        rec.op("vector", lambda e: e.memset(runmn[:, 0:8 * CA], 1e30),
               sig="b1_zn", wait=("b1_fmn",))
        local_gmax("arg", rmax_gi, 1, ("b1_fmn", "ar0_m", "ar0_di"),
                   ("ar0_fin",))
        do_cc("arg", 1, ("arg_m",))
        rec.op("vector", lambda e: e.tensor_tensor(
            sc(9), ccr_sb[0:1, 1:2], pp1[0:1, 3:4], ALU.mult),
            sig="arg_t3", wait=("arg_do", "ar0_fin"))
        rec.op("vector", lambda e: e.tensor_scalar(
            sc(9), sc(9), 8.0 / QMAX, 1e-12, ALU.mult, ALU.max),
            sig="arg_t4", wait=("arg_t3",))
        rec.op("vector", lambda e: e.reciprocal(sc(11), sc(9)),
               sig="arg_t6", wait=("arg_t4",))
        rec.op("vector", lambda e: e.tensor_tensor(
            sc(0), pp1[0:1, 3:4], sc(11), ALU.mult),
            sig="arg_t8a", wait=("arg_t6",))
        rec.op("vector", lambda e: e.tensor_scalar(
            sc(0), sc(0), 8.0, None, ALU.mult),
            sig="arg_t8", wait=("arg_t8a",))
        rec.op("vector", lambda e: e.tensor_copy(sc(1), sc(9)),
               sig="arg_t9", wait=("arg_t4",))
        rec.op("vector", lambda e: e.tensor_scalar(
            sc(2), sc(9), -SHIFT, None, ALU.mult),
            sig="arg_tA", wait=("arg_t4",))
        rec.op("vector", lambda e: e.tensor_copy(sc(5), sc(9)),
            sig="arg_sc", wait=("arg_tA", "arg_t8", "arg_t9"))
        finish("arg", pp2)
        esB1.close()

        esC1 = ExitStack()
        ecc = esC1.enter_context
        gi_r = ecc(nc.sbuf_tensor("gi_r", [128, NI, CC], F16))
        gi_u = ecc(nc.sbuf_tensor("gi_u", [128, NI, CC], U16))
        sg_t = ecc(nc.sbuf_tensor("sg_t", [128, NI, CC], F16))
        sigi = ecc(nc.sbuf_tensor("sigi", [128, NI, CC], U16))
        gi_s = ecc(nc.sbuf_tensor("gi_s", [128, NI, CC], F16))
        g2_t = ecc(nc.sbuf_tensor("g2_t", [128, NI, 2 * CC], F16))

        C1W = 1536

        def c1_chunk(c):
            half = c % 2
            w = ["arg_fin", f"b1_sp{(c // 2) * 3 + 0}",
                 f"b1_sp{(c // 2) * 3 + 1}", f"b1_sp{(c // 2) * 3 + 2}"]
            if c >= 1:
                w.append(f"c1_u{c-1}")
            rec.op("sync", lambda e, c=c: e.dma_start(
                out=gi_r[:, :, :],
                in_=gis.ap().rearrange("(i p) T -> p i T", p=128)[
                    :, :, c * CC:(c + 1) * CC]),
                sig=f"c1_ld{c}", wait=tuple(w))
            flu = gi_u.ap().rearrange("p i t -> p (i t)")
            fls = sg_t.ap().rearrange("p i t -> p (i t)")
            fli = sigi.ap().rearrange("p i t -> p (i t)")
            fgs = gi_s.ap().rearrange("p i t -> p (i t)")
            flr = gi_r.ap().rearrange("p i t -> p (i t)")
            w2 = [f"c1_ld{c}"]
            if c >= 1:
                w2.append(f"c1_g2p{c-1}")
            rec.op("vector", lambda e, flu=flu, flr=flr: e.tensor_scalar(
                flu, flr, pp2[:, 0:1], ROFF, ALU.mult, ALU.add),
                sig=f"c1_u{c}", wait=tuple(w2))
            w2 = [f"c1_u{c}"]
            if c >= 1:
                w2.append(f"c1_si{c-1}")
            rec.op("scalar", lambda e, flu=flu, fls=fls: e.activation(
                fls, flu, AF.Sigmoid, bias=pp2[:, 2:3], scale=pp2[:, 1:2]),
                sig=f"c1_sg{c}", wait=tuple(w2))
            w2 = [f"c1_sg{c}"]
            if c >= 1:
                w2.append(f"c1_g2p{c-1}")
            rec.op("scalar", lambda e, fls=fls, fli=fli: e.activation(
                fli, fls, AF.Copy, bias=0.0, scale=65536.0),
                sig=f"c1_si{c}", wait=tuple(w2))
            w2 = [f"c1_u{c}"]
            if c >= 1:
                w2.append(f"c1_g2p{c-1}")
            rec.op("vector", lambda e, fgs=fgs, flu=flu: e.tensor_scalar(
                fgs, flu, -SHIFT, 2.0 ** -16, ALU.add, ALU.mult),
                sig=f"c1_gs{c}", wait=tuple(w2))
            w2 = [f"c1_si{c}", f"c1_gs{c}"]
            if c >= 2:
                w2 += [f"c1_g2{(c // 2 - 1) * 2}", f"c1_mx{c-2}"]
            rec.op("vector", lambda e, half=half:
                   e.tensor_tensor(
                       g2_t[:, :, half * CC:(half + 1) * CC],
                       gi_s[:, :, :], sigi[:, :, :], ALU.mult),
                   sig=f"c1_g2p{c}", wait=tuple(w2))
            w2 = [f"c1_g2p{c}", "b1_zx", "b1_zn"]
            if c >= 1:
                w2.append(f"c1_mx{c-1}")
            base = 8 * CA
            for q in range(4):
                s_ap = g2_t[:, q * 6:(q + 1) * 6,
                            half * CC:(half + 1) * CC]
                rmxs = runmx[:, base:base + C1W].rearrange(
                    "p (i t) -> p i t", t=CC)
                rmns = runmn[:, base:base + C1W].rearrange(
                    "p (i t) -> p i t", t=CC)
                rec.op("vector", lambda e, s_ap=s_ap, rmxs=rmxs:
                       e.tensor_tensor(rmxs, s_ap, rmxs, ALU.max),
                       sig=f"c1_mxa{c}_{q}" if q < 3 else f"c1_mxa{c}",
                       wait=tuple(w2) if q == 0 else
                       (f"c1_mxa{c}_{q-1}",))
            last = f"c1_mxa{c}"
            for q in range(4):
                s_ap = g2_t[:, q * 6:(q + 1) * 6,
                            half * CC:(half + 1) * CC]
                rmns = runmn[:, base:base + C1W].rearrange(
                    "p (i t) -> p i t", t=CC)
                rec.op("vector", lambda e, s_ap=s_ap, rmns=rmns:
                       e.tensor_tensor(rmns, s_ap, rmns, ALU.min),
                       sig=f"c1_mxn{c}_{q}" if q < 3 else f"c1_mx{c}",
                       wait=(last,) if q == 0 else (f"c1_mxn{c}_{q-1}",))
            if half == 1:
                rec.op("gdma", lambda e, c=c: e.dma_start(
                    out=g2s.ap().rearrange("(i p) T -> p i T", p=128)[
                        :, :, (c - 1) * CC:(c + 1) * CC],
                    in_=g2_t[:, :, :]),
                    sig=f"c1_g2{c-1}", wait=(f"c1_g2p{c}", f"c1_mx{c}"))

        b_prep_chunk("b2", 0, "u")
        for c in range(NCA):
            if c + 1 < NCA:
                b_prep_chunk("b2", c + 1, "u")
            b_phase_chunk("b2", c, wuT, uis, "u")
            c1_chunk(2 * c)
            c1_chunk(2 * c + 1)

        rec.op("vector", lambda e: e.tensor_reduce(
            rmax_ui[:, 0:1], runmx[:, 0:8 * CA], AX.X, ALU.max),
            sig="b2_fmx", wait=(f"b2_mx{NCA*3-1}",))
        rec.op("vector", lambda e: e.tensor_reduce(
            rmax_ui[:, 1:2], runmn[:, 0:8 * CA], AX.X, ALU.min, negate=True),
            sig="b2_fmn", wait=("b2_fmx",))
        rec.op("vector", lambda e: e.tensor_reduce(
            rmax_g2[:, 0:1], runmx[:, 8 * CA:8 * CA + C1W], AX.X, ALU.max),
            sig="c1_fmx", wait=(f"c1_mx{2*NCA-1}",))
        rec.op("vector", lambda e: e.tensor_reduce(
            rmax_g2[:, 1:2], runmn[:, 8 * CA:8 * CA + C1W], AX.X, ALU.min,
            negate=True), sig="c1_fmn", wait=("c1_fmx",))
        rec.op("vector", lambda e: e.memset(runmx[:], -1e30), sig="b2_zx",
               wait=("b2_fmx", "c1_fmx"))
        rec.op("vector", lambda e: e.memset(runmn[:], 1e30), sig="b2_zn",
               wait=("b2_fmn", "c1_fmn"))

        local_gmax("aru", rmax_ui, 2, ("b2_fmn", "arg_m", "arg_di"),
                   ("arg_fin",))
        local_gmax("ar2", rmax_g2, 3, ("aru_m", "c1_fmn"), ("aru_ev",))
        do_cc("ar2", 2, ("aru_m", "ar2_m"))
        rec.op("vector", lambda e: e.tensor_tensor(
            sc(8), ccr_sb[0:1, 2:3], pp1[0:1, 3:4], ALU.mult),
            sig="ar2_t1", wait=("ar2_do", "ar0_fin", "arg_fin"))
        rec.op("vector", lambda e: e.tensor_scalar(
            sc(8), sc(8), 8.0 / QMAX, 1e-12, ALU.mult, ALU.max),
            sig="ar2_t2", wait=("ar2_t1",))
        rec.op("vector", lambda e: e.reciprocal(sc(10), sc(8)),
               sig="ar2_t5", wait=("ar2_t2",))
        rec.op("vector", lambda e: e.tensor_tensor(
            sc(4), pp1[0:1, 3:4], sc(10), ALU.mult),
            sig="ar2_t7a", wait=("ar2_t5",))
        rec.op("vector", lambda e: e.tensor_scalar(
            sc(4), sc(4), 8.0, None, ALU.mult),
            sig="ar2_t7", wait=("ar2_t7a",))
        rec.op("vector", lambda e: e.tensor_copy(sc(6), sc(8)),
               sig="ar2_t7s", wait=("ar2_t2",))
        rec.op("vector", lambda e: e.tensor_tensor(
            sc(11), ccr_sb[0:1, 3:4], pp2[0:1, 5:6], ALU.mult),
            sig="ar2_t8", wait=("ar2_do", "arg_fin"))
        rec.op("vector", lambda e: e.tensor_scalar(
            sc(11), sc(11), 1.0 / QMAX, 1e-12, ALU.mult, ALU.max),
            sig="ar2_t9", wait=("ar2_t8",))
        rec.op("vector", lambda e: e.tensor_copy(sc(3), sc(11)),
               sig="ar2_tA", wait=("ar2_t9",))
        rec.op("vector", lambda e: e.reciprocal(sc(12), sc(11)),
               sig="ar2_tB", wait=("ar2_t9",))
        rec.op("vector", lambda e: e.tensor_tensor(
            sc(0), pp2[0:1, 5:6], sc(12), ALU.mult),
            sig="ar2_sc", wait=("ar2_tB", "ar2_t7", "ar2_t7s", "ar2_tA"))
        finish("ar2", pp3)
        esC1.close()
        esBc.close()
        BAR_C1 = ("ar2_tr", "ar2_sc", f"c1_sg{2*NCA-1}", "ar2_do",
                  "ar2_fin")

        esC2 = ExitStack()
        ec2 = esC2.enter_context
        ui_r = ec2(nc.sbuf_tensor("ui_r", [128, 2, NI, 2 * CC], F16))
        g2_r = ec2(nc.sbuf_tensor("g2_r", [128, 2, NI, 2 * CC], F16))
        ui_u = ec2(nc.sbuf_tensor("ui_u", [128, NI, CC], U16))
        ui_s = ec2(nc.sbuf_tensor("ui_s", [128, NI, CC], F16))
        g2i_u = ec2(nc.sbuf_tensor("g2i_u", [128, NI, CC], U16))
        g2i_s = ec2(nc.sbuf_tensor("g2i_s", [128, NI, CC], F16))
        op_t = ec2(nc.sbuf_tensor("op_t", [128, NI, 2 * CC], F16))

        def c2_chunk(c):
            lb = (c // 2) % 2
            half = c % 2
            if half == 0:
                w = ["ar2_fin", f"b2_sp{(c // 2) * 3 + 0}",
                     f"b2_sp{(c // 2) * 3 + 1}",
                     f"b2_sp{(c // 2) * 3 + 2}"]
                if c == 0:
                    w += list(BAR_C1)
                if c >= 4:
                    w.append(f"c2_uu{c-3}")
                rec.op("sync", lambda e, c=c, lb=lb: e.dma_start(
                    out=ui_r[:, lb, :, :],
                    in_=uis.ap().rearrange("(i p) T -> p i T", p=128)[
                        :, :, c * CC:(c + 2) * CC]),
                    sig=f"c2_lu{c}", wait=tuple(w))
                w = ["ar2_fin", f"c1_g2{c}"]
                if c == 0:
                    w += list(BAR_C1)
                if c >= 4:
                    w.append(f"c2_gu{c-3}")
                rec.op("sync", lambda e, c=c, lb=lb: e.dma_start(
                    out=g2_r[:, lb, :, :],
                    in_=g2s.ap().rearrange("(i p) T -> p i T", p=128)[
                        :, :, c * CC:(c + 2) * CC]),
                    sig=f"c2_lg{c}", wait=tuple(w))
            flu = ui_u.ap().rearrange("p i t -> p (i t)")
            fls = ui_s.ap().rearrange("p i t -> p (i t)")
            flg = g2i_u.ap().rearrange("p i t -> p (i t)")
            fgi = g2i_s.ap().rearrange("p i t -> p (i t)")
            w2 = [f"c2_lu{c - half}"]
            if c >= 1:
                w2.append(f"c2_us{c-1}")
            rec.op("scalar", lambda e, lb=lb, half=half, flu=flu:
                   e.activation(
                       flu, ui_r[:, lb, :, half * CC:(half + 1) * CC],
                       AF.Copy, bias=ROFF, scale=pp3[:, 4:5]),
                   sig=f"c2_uu{c}", wait=tuple(w2))
            w2 = [f"c2_uu{c}"]
            if c >= 1:
                w2.append(f"c2_op{c-1}")
            rec.op("vector", lambda e, flu=flu, fls=fls: e.tensor_scalar(
                fls, flu, -SHIFT, 2.0 ** -13, ALU.add, ALU.mult),
                sig=f"c2_us{c}", wait=tuple(w2))
            w2 = [f"c2_lg{c - half}"]
            if c >= 1:
                w2.append(f"c2_gs{c-1}")
            rec.op("scalar", lambda e, lb=lb, half=half, flg=flg:
                   e.activation(
                       flg, g2_r[:, lb, :, half * CC:(half + 1) * CC],
                       AF.Copy, bias=ROFF, scale=pp3[:, 0:1]),
                   sig=f"c2_gu{c}", wait=tuple(w2))
            w2 = [f"c2_gu{c}"]
            if c >= 1:
                w2.append(f"c2_op{c-1}")
            rec.op("vector", lambda e, fgi=fgi, flg=flg: e.tensor_scalar(
                fgi, flg, -SHIFT, 2.0 ** -16, ALU.add, ALU.mult),
                sig=f"c2_gs{c}", wait=tuple(w2))
            w2 = [f"c2_gs{c}", f"c2_us{c}"]
            if c >= 2:
                w2 += [f"c2_sp{(c // 2 - 1) * 2}", f"c2_mx{c-2}"]
            rec.op("vector", lambda e, half=half:
                   e.tensor_tensor(
                       op_t[:, :, half * CC:(half + 1) * CC],
                       g2i_s[:, :, :], ui_s[:, :, :], ALU.mult),
                   sig=f"c2_op{c}", wait=tuple(w2))
            W2 = NI * CC // 2
            w2 = [f"c2_op{c}", "b2_zx", "b2_zn"]
            if c >= 1:
                w2.append(f"c2_mx{c-1}")
            rmxs = runmx[:, 0:W2].rearrange("p (i t) -> p i t", t=CC)
            rmns = runmn[:, 0:W2].rearrange("p (i t) -> p i t", t=CC)
            for q in range(2):
                s_ap = op_t[:, q * (NI // 2):(q + 1) * (NI // 2),
                            half * CC:(half + 1) * CC]
                rec.op("vector", lambda e, s_ap=s_ap, rmxs=rmxs:
                       e.tensor_tensor(rmxs, s_ap, rmxs, ALU.max),
                       sig=f"c2_mxa{c}_{q}",
                       wait=tuple(w2) if q == 0 else (f"c2_mxa{c}_0",))
            for q in range(2):
                s_ap = op_t[:, q * (NI // 2):(q + 1) * (NI // 2),
                            half * CC:(half + 1) * CC]
                rec.op("vector", lambda e, s_ap=s_ap, rmns=rmns:
                       e.tensor_tensor(rmns, s_ap, rmns, ALU.min),
                       sig=f"c2_mxn{c}_{q}" if q == 0 else f"c2_mx{c}",
                       wait=(f"c2_mxa{c}_1",) if q == 0
                       else (f"c2_mxn{c}_0",))
            if half == 1:
                rec.op("gdma", lambda e, c=c: e.dma_start(
                    out=ops_.ap().rearrange("(i p) T -> p i T", p=128)[
                        :, :, (c - 1) * CC:(c + 1) * CC],
                    in_=op_t[:, :, :]),
                    sig=f"c2_sp{c-1}", wait=(f"c2_op{c}", f"c2_mx{c}"))

        for c in range(NCC):
            c2_chunk(c)

        fold_runs("ar3f", NI * CC // 2, rmax_o, (f"c2_mx{NCC-1}",))
        local_gmax("ar3", rmax_o, 4, ("ar3f_fmn", "ar2_m", "ar2_di"),
                   ("ar2_fin",))
        do_cc("ar3", 3, ("ar3_m",))
        rec.op("vector", lambda e: e.tensor_tensor(
            sc(9), pp3[0:1, 6:7], pp3[0:1, 3:4], ALU.mult),
            sig="ar3_t2", wait=("ar3_do", "ar2_fin"))
        rec.op("vector", lambda e: e.tensor_tensor(
            sc(10), ccr_sb[0:1, 4:5], sc(9), ALU.mult),
            sig="ar3_t3", wait=("ar3_t2",))
        rec.op("vector", lambda e: e.tensor_scalar(
            sc(10), sc(10), (2.0 ** 29) / QMAX, 1e-12, ALU.mult, ALU.max),
            sig="ar3_t4", wait=("ar3_t3",))
        rec.op("vector", lambda e: e.reciprocal(sc(11), sc(10)),
               sig="ar3_t5", wait=("ar3_t4",))
        rec.op("vector", lambda e: e.tensor_tensor(
            sc(0), sc(9), sc(11), ALU.mult),
            sig="ar3_t6a", wait=("ar3_t5",))
        rec.op("vector", lambda e: e.tensor_scalar(
            sc(0), sc(0), 2.0 ** 29, None, ALU.mult),
            sig="ar3_t6", wait=("ar3_t6a",))
        rec.op("vector", lambda e: e.tensor_copy(sc(3), sc(10)),
               sig="ar3_sc", wait=("ar3_t6",))
        finish("ar3", pp4)
        esC2.close()
        esRun.close()
        if max_phase < 4:
            rec.op("sync", lambda e: None, wait=("ar3_fin",))
            with nc.Block() as block:
                rec.emit(block, sems)
            return nc
        BAR_C2 = ("ar3_tr", "ar3_sc", f"c1_sg{NCC-1}", "ar3_do", "ar3_fin")

        esD = ExitStack()
        ed = esD.enter_context
        wdT = ed(nc.sbuf_tensor("wdT", [128, NI, DM], F16))
        o_r = ed(nc.sbuf_tensor("o_r", [128, 2, NI, 2 * CC], F16))
        oi_u = ed(nc.sbuf_tensor("oi_u", [128, NI, CC], U16))
        oi_b = ed(nc.sbuf_tensor("oi_b", [128, 2, NI, CC], F16))
        x1T = ed(nc.sbuf_tensor("x1T", [128, 4, 2, DM], F16))
        x2_t = ed(nc.sbuf_tensor("x2_t", [128, 2, 2, DM], F16))
        o_sb = ed(nc.sbuf_tensor("o_sb", [128, 2, 2, DM], F32))
        dcol = ed(nc.sbuf_tensor("dcol", [128, 8], F32))
        dsq = ed(nc.sbuf_tensor("dsq", [128, 2, DM], F16))

        for it in range(NI):
            s = it // ND
            q = it % ND
            rec.op("synct", lambda e, s=s, q=q, it=it: e.dma_start_transpose(
                wdT[:, it, :],
                wag.ap()[(48 + 8 * s) * 128:(48 + 8 * s + 8) * 128,
                         q * 128:(q + 1) * 128]),
                sig=f"d_wd{it}", wait=("w_ag",) + BAR_C2)

        def d_chunk(c):
            b = c % 2
            lb = (c // 2) % 2
            half = c % 2
            if half == 0:
                w = ["ar3_fin", f"c2_sp{c}"]
                if c == 0:
                    w += list(BAR_C2)
                if c >= 4:
                    w.append(f"d_ou{c-3}")
                rec.op("sync", lambda e, c=c, lb=lb: e.dma_start(
                    out=o_r[:, lb, :, :],
                    in_=ops_.ap().rearrange("(i p) T -> p i T", p=128)[
                        :, :, c * CC:(c + 2) * CC]),
                    sig=f"d_lo{c}", wait=tuple(w))
            xb = c % 4
            for tt in range(2):
                wx = ["ar3_fin", f"a_sp{(2 * c + tt) // 4}"]
                if c >= 4:
                    wx.append(f"d_ev{((c-4)*2+tt)*2+1}")
                rec.op("synct",
                       lambda e, c=c, xb=xb, tt=tt:
                       e.dma_start_transpose(
                           x1T[:, xb, tt, :],
                           x1s.ap()[:, (2 * c + tt) * 128:
                                    (2 * c + tt + 1) * 128]),
                       sig=f"d_x1{c}_{tt}", wait=tuple(wx))
            flu = oi_u.ap().rearrange("p i t -> p (i t)")
            flb = oi_b.ap().rearrange("p a i t -> p a (i t)")
            w2 = [f"d_lo{c - half}"]
            if c >= 1:
                w2.append(f"d_oib{c-1}")
            rec.op("vector", lambda e, lb=lb, half=half, flu=flu:
                   e.tensor_scalar(
                       flu, o_r[:, lb, :, half * CC:(half + 1) * CC],
                       pp4[:, 0:1], ROFF, ALU.mult, ALU.add),
                   sig=f"d_ou{c}", wait=tuple(w2))
            w2 = [f"d_ou{c}"]
            if c >= 2:
                w2.append(f"d_mm{((c-2)*2+1)*2+1}")
            rec.op("vector", lambda e, b=b, flb=flb, flu=flu:
                   e.tensor_scalar(
                       flb[:, b], flu, -SHIFT, None, ALU.add),
                   sig=f"d_oib{c}", wait=tuple(w2))
            for tt in range(2):
                for dh in range(2):
                    g = (c * 2 + tt) * 2 + dh
                    pb = P[g % 4]
                    wt = [f"d_oib{c}", f"d_wd{NI-1}"]
                    if g >= 4:
                        wt.append(f"d_ev{g-4}")
                    for it in range(NI):
                        rec.op("tensor",
                               lambda e, b=b, tt=tt, dh=dh, it=it, pb=pb:
                               e.matmul(
                                   pb[:],
                                   oi_b[:, b, it, tt * 128:(tt + 1) * 128],
                                   wdT[:, it, dh * 512:(dh + 1) * 512],
                                   start=(it == 0), stop=(it == NI - 1)),
                               sig=f"d_mm{g}" if it == NI - 1 else None,
                               wait=tuple(wt) if it == 0 else ())
                    w2 = [f"d_mm{g}", f"d_x1{c}_{tt}"]
                    if c >= 2:
                        w2.append(f"d_ap{c-2}_{tt}")
                    rec.op("vector", lambda e, b=b, xb=xb, tt=tt, dh=dh,
                           pb=pb: e.scalar_tensor_tensor(
                               x2_t[:, b, tt, dh * 512:(dh + 1) * 512],
                               pb[:], pp4[:, 3:4],
                               x1T[:, xb, tt, dh * 512:(dh + 1) * 512],
                               ALU.mult, ALU.add),
                           sig=f"d_ev{g}", wait=tuple(w2))
                gl = (c * 2 + tt) * 2 + 1
                p_ = c * 2 + tt
                prev_ap = (f"d_ap{(p_-1) // 2}_{(p_-1) % 2}",) if p_ else ()
                rec.op("vector", lambda e, b=b, tt=tt: e.reduce_sum(
                    dcol[:, 0:1], x2_t[:, b, tt, :], AX.X),
                    sig=f"d_sum{c}_{tt}", wait=(f"d_ev{gl}",) + prev_ap)
                rec.op("scalar", lambda e, b=b, tt=tt: e.activation(
                    dsq[:, tt, :], x2_t[:, b, tt, :], AF.Square,
                    accum_out=dcol[:, 1:2]),
                    sig=f"d_ssq{c}_{tt}", wait=(f"d_ev{gl}",) + prev_ap)
                rec.op("vector", lambda e: e.tensor_scalar(
                    dcol[:, 2:3], dcol[:, 0:1], 1.0 / DM, None, ALU.mult),
                    sig=f"d_m{c}_{tt}",
                    wait=(f"d_sum{c}_{tt}", f"d_ssq{c}_{tt}"))
                rec.op("vector", lambda e: e.tensor_scalar(
                    dcol[:, 3:4], dcol[:, 1:2], 1.0 / DM, None, ALU.mult),
                    sig=f"d_ms{c}_{tt}", wait=(f"d_ssq{c}_{tt}",))
                rec.op("vector", lambda e: e.tensor_tensor(
                    dcol[:, 4:5], dcol[:, 2:3], dcol[:, 2:3], ALU.mult),
                    sig=f"d_sqm{c}_{tt}", wait=(f"d_m{c}_{tt}",))
                rec.op("vector", lambda e: e.tensor_tensor(
                    dcol[:, 4:5], dcol[:, 3:4], dcol[:, 4:5], ALU.subtract),
                    sig=f"d_var{c}_{tt}",
                    wait=(f"d_ms{c}_{tt}", f"d_sqm{c}_{tt}"))
                rec.op("scalar", lambda e: e.activation(
                    dcol[:, 5:6], dcol[:, 4:5], AF.Sqrt,
                    bias=eps_sb[:, 0:1], scale=1.0), sig=f"d_sd{c}_{tt}",
                    wait=(f"d_var{c}_{tt}", "epsz"))
                rec.op("vector", lambda e: e.reciprocal(
                    dcol[:, 6:7], dcol[:, 5:6]),
                    sig=f"d_rs{c}_{tt}", wait=(f"d_sd{c}_{tt}",))
                rec.op("vector", lambda e: e.tensor_tensor(
                    dcol[:, 7:8], dcol[:, 2:3], dcol[:, 6:7], ALU.mult),
                    sig=f"d_nm0{c}_{tt}", wait=(f"d_rs{c}_{tt}",))
                rec.op("vector", lambda e: e.tensor_scalar(
                    dcol[:, 7:8], dcol[:, 7:8], -1.0, None, ALU.mult),
                    sig=f"d_nm{c}_{tt}", wait=(f"d_nm0{c}_{tt}",))
                wap = [f"d_nm{c}_{tt}"]
                if c >= 2:
                    wap.append(f"d_st{c-2}")
                rec.op("vector", lambda e, b=b, tt=tt: e.tensor_scalar(
                    o_sb[:, b, tt, :], x2_t[:, b, tt, :], dcol[:, 6:7],
                    dcol[:, 7:8], ALU.mult, ALU.add),
                    sig=f"d_ap{c}_{tt}", wait=tuple(wap))
            rec.op("sync", lambda e, c=c, b=b: e.dma_start(
                out=out_h[c * CC:(c + 1) * CC, :].rearrange(
                    "(a p) d -> p a d", p=128),
                in_=o_sb[:, b, :, :]),
                sig=f"d_st{c}", wait=(f"d_ap{c}_0", f"d_ap{c}_1"))

        for c in range(NCC):
            d_chunk(c)

        if "scal" in dbg:
            rec.op("sync", lambda e: e.dma_start(
                out=dbg_scal.ap(), in_=sc_sb[:]), sig="dbg_sc",
                wait=("ar3_sc",))

        rec.op("sync", lambda e: None,
               wait=tuple(f"d_st{c}" for c in range(NCC)))

        esD.close()
        with nc.Block() as block:
            rec.emit(block, sems, prefix=prefix)

    return nc


_NC_CACHE = {}


def _get_nc(dbg=()):
    key = tuple(sorted(dbg))
    if key not in _NC_CACHE:
        _NC_CACHE[key] = build(dbg)
    return _NC_CACHE[key]


def _wslice(wu, wg, wd, r):
    tiles = []
    for g in range(9 * r, 9 * r + 9):
        if g < 24:
            tiles.append(wu[g * 128:(g + 1) * 128, :])
        elif g < 48:
            t = g - 24
            tiles.append(wg[t * 128:(t + 1) * 128, :])
        else:
            s, t = (g - 48) // 8, (g - 48) % 8
            tiles.append(wd[t * 128:(t + 1) * 128, s * DM:(s + 1) * DM])
    return np.ascontiguousarray(np.concatenate(tiles, axis=0))


def kernel(L0_attn_u, L0_attn_log_lambda, L0_attn_B_w, L0_attn_C_w,
           L0_attn_D, L0_ffn_w_gate, L0_ffn_w_up, L0_ffn_w_down):
    u = np.ascontiguousarray(np.asarray(L0_attn_u, dtype=np.float32))
    ll = np.ascontiguousarray(np.asarray(L0_attn_log_lambda,
                                         dtype=np.float32))
    bw = np.ascontiguousarray(np.asarray(L0_attn_B_w, dtype=np.float32))
    cw = np.ascontiguousarray(np.asarray(L0_attn_C_w, dtype=np.float32))
    dv = np.ascontiguousarray(np.asarray(L0_attn_D, dtype=np.float32))
    wg = np.ascontiguousarray(np.asarray(L0_ffn_w_gate, dtype=np.float32))
    wu = np.ascontiguousarray(np.asarray(L0_ffn_w_up, dtype=np.float32))
    wd = np.ascontiguousarray(np.asarray(L0_ffn_w_down, dtype=np.float32))

    nc = _get_nc()
    in_maps = []
    for r in range(8):
        in_maps.append(dict(u=np.ascontiguousarray(u[r]), ll=ll, bw=bw,
                            cw=cw, dv=dv, wslice=_wslice(wu, wg, wd, r)))
    res = run_bass_kernel_spmd(nc, in_maps, core_ids=list(range(8)))
    out = np.stack([res.results[r]["out"] for r in range(8)], axis=0)
    return out.astype(np.float32)
